# version 07
# baseline (speedup 1.0000x reference)
import numpy as np
import ml_dtypes
from contextlib import ExitStack

import concourse.bass as bass
import concourse.mybir as mybir
from concourse.bass_utils import run_bass_kernel_spmd

F32 = mybir.dt.float32
BF16 = mybir.dt.bfloat16
U8 = mybir.dt.uint8
AF = mybir.ActivationFunctionType
ALU = mybir.AluOpType
AX = mybir.AxisListType
BF = ml_dtypes.bfloat16

T = 2048
D = 2048
NT = D // 128
NTT = T // 128
NEG_HUGE = float(np.finfo(np.float32).min) / 4.0
LN_EPS = 1e-5
SCALE_L0 = 1.0 / float(np.sqrt(128.0))
ARENA_BYTES = 136 * 1024
FFN0_SOURCES = list(range(12))
FFN0_SKIP_DOWN = False
FFN0_SKIP_H = False
DBG_DUMP = set()



def _bf(a):
    return np.ascontiguousarray(np.asarray(a).astype(BF))


def _f32(a):
    return np.ascontiguousarray(np.asarray(a, dtype=np.float32))


def pack_dmajor(wT, chunk):
    Din, OF = wT.shape
    nt = Din // 128
    assert OF % chunk == 0
    nch = OF // chunk
    a = wT.reshape(nt, 128, nch, chunk).transpose(2, 1, 0, 3).reshape(nch, 128, nt * chunk)
    return np.ascontiguousarray(a)


def pack_vec_ptiles(v):
    n = v.shape[0] // 128
    return np.ascontiguousarray(v.reshape(n, 128).T)


def rope_tables():
    half = 64
    inv = 1.0 / (1000000.0 ** (np.arange(half, dtype=np.float64) / half))
    ang = np.outer(inv, np.arange(T, dtype=np.float64))
    cos = np.cos(ang).astype(np.float32)
    sin = np.sin(ang).astype(np.float32)
    cosT = np.concatenate([cos, cos], axis=0)
    sinS = np.concatenate([-sin, sin], axis=0)
    return np.ascontiguousarray(cosT), np.ascontiguousarray(sinS)


CONSTS = [
    ("c_ident", (128, 128), "bf16"),
    ("c_identf", (128, 128), "f32"),
    ("c_mask", (128, 128), "f32"),
    ("c_ones_row", (1, 128), "bf16"),
    ("c_ones_col", (128, 1), "bf16"),
    ("c_blockones2", (2, 64), "bf16"),
]
L0_CONSTS = [
    ("b_q", (128, 16), "f32"),
    ("b_k", (128, 4), "f32"),
    ("b_v", (1, 512), "bf16"),
]
FFN0_CONSTS = [
    ("w_rt", (128, 128), "f32"),
    ("w_sg", (128, 16), "f32"),
]
L1A_CONSTS = [
    ("b1_p", (128, 16), "f32"),
]
L1F_CONSTS = [
    ("s1_g", (128, 44), "f32"),
    ("s1_u", (128, 44), "f32"),
    ("s1_d", (128, 16), "f32"),
]


def host_pack(inputs):
    pk = {}
    pk["c_ident"] = np.eye(128, dtype=BF)
    pk["c_identf"] = np.eye(128, dtype=np.float32)
    m = np.zeros((128, 128), np.float32)
    m[np.triu_indices(128, 1)] = NEG_HUGE
    pk["c_mask"] = m
    pk["c_ones_row"] = np.ones((1, 128), BF)
    pk["c_ones_col"] = np.ones((128, 1), BF)
    b2 = np.zeros((2, 64), BF)
    b2[0, 0:32] = 1
    b2[1, 32:64] = 1
    pk["c_blockones2"] = b2
    cosT, sinS = rope_tables()
    pk["c_cos"] = _bf(cosT)
    pk["c_sin"] = _bf(sinS)

    pk["w_q"] = pack_dmajor(_bf(inputs["L0_attn_Wq"]), 128)
    pk["w_k"] = pack_dmajor(_bf(inputs["L0_attn_Wk"]), 128)
    pk["w_v"] = pack_dmajor(_bf(inputs["L0_attn_Wv"]), 128)
    pk["w_o"] = pack_dmajor(_bf(inputs["L0_attn_Wo"]), 128)
    pk["b_q"] = pack_vec_ptiles(_f32(inputs["L0_attn_bq"]))
    pk["b_k"] = pack_vec_ptiles(_f32(inputs["L0_attn_bk"]))
    pk["b_v"] = _bf(_f32(inputs["L0_attn_bv"]).reshape(1, 512))

    rw = _f32(inputs["L0_ffn_router_weight"])
    pk["w_rt"] = np.ascontiguousarray(pack_dmajor(np.ascontiguousarray(rw.T), 8)[0])
    gu = _f32(inputs["L0_ffn_gate_up_proj"])
    dn = _f32(inputs["L0_ffn_down_proj"])
    for e in range(8):
        guT = np.ascontiguousarray(gu[e].T)
        pk[f"w_fg{e + 4}"] = pack_dmajor(np.ascontiguousarray(guT[:, :1408]), 128)
        pk[f"w_fu{e + 4}"] = pack_dmajor(np.ascontiguousarray(guT[:, 1408:]), 128)
        pk[f"w_fd{e + 4}"] = pack_dmajor(np.ascontiguousarray(dn[e].T), 128)
    shgT = np.ascontiguousarray(_f32(inputs["L0_ffn_sh_gate_w"]).T)
    shuT = np.ascontiguousarray(_f32(inputs["L0_ffn_sh_up_w"]).T)
    shdT = np.ascontiguousarray(_f32(inputs["L0_ffn_sh_down_w"]).T)
    for s in range(4):
        pk[f"w_fg{s}"] = pack_dmajor(np.ascontiguousarray(shgT[:, s * 1408:(s + 1) * 1408]), 128)
        pk[f"w_fu{s}"] = pack_dmajor(np.ascontiguousarray(shuT[:, s * 1408:(s + 1) * 1408]), 128)
        pk[f"w_fd{s}"] = pack_dmajor(np.ascontiguousarray(shdT[s * 1408:(s + 1) * 1408, :]), 128)
    pk["w_sg"] = pack_vec_ptiles(_f32(inputs["L0_ffn_shared_gate_w"]).reshape(-1))

    wqkvT = np.ascontiguousarray(_f32(inputs["L1_attn_w_qkv"]).T)
    pk["w1_q"] = pack_dmajor(np.ascontiguousarray(wqkvT[:, 0:2048]), 512)
    pk["w1_k"] = pack_dmajor(np.ascontiguousarray(wqkvT[:, 2048:4096]), 256)
    pk["w1_v"] = pack_dmajor(np.ascontiguousarray(wqkvT[:, 4096:6144]), 256)
    pk["w1_p"] = pack_dmajor(np.ascontiguousarray(_f32(inputs["L1_attn_w_proj"]).T), 512)
    pk["b1_p"] = pack_vec_ptiles(_f32(inputs["L1_attn_b_proj"]))

    pk["w1_g"] = pack_dmajor(np.ascontiguousarray(_f32(inputs["L1_ffn_gate_w"]).T), 128)
    pk["w1_u"] = pack_dmajor(np.ascontiguousarray(_f32(inputs["L1_ffn_up_w"]).T), 128)
    pk["w1_d"] = pack_dmajor(np.ascontiguousarray(_f32(inputs["L1_ffn_down_w"]).T), 128)
    pk["s1_g"] = pack_vec_ptiles(_f32(inputs["L1_ffn_gate_s"]))
    pk["s1_u"] = pack_vec_ptiles(_f32(inputs["L1_ffn_up_s"]))
    pk["s1_d"] = pack_vec_ptiles(_f32(inputs["L1_ffn_down_s"]))
    return pk



_DT = {"f32": F32, "bf16": BF16}


class Sem:
    def __init__(self, h, step=1):
        self.h = h
        self.n = 0
        self.step = step

    def inc(self):
        self.n += self.step
        return self.n


class EP:
    def __init__(self, name=""):
        self.name = name
        self.ops = []
        self.tag = ""

    def __call__(self, fn):
        self.ops.append((fn, self.tag))

    def replay(self, eng):
        for idx, (fn, tag) in enumerate(self.ops):
            try:
                fn(eng)
            except Exception as ex:
                dd = []
                for dv in (fn.__defaults__ or ()):
                    try:
                        dd.append(getattr(dv, "concise", lambda: repr(dv))())
                    except Exception:
                        dd.append(str(type(dv)))
                raise RuntimeError(
                    f"engine {self.name} op {idx} tag={tag}: {ex}\nargs={dd}") from ex


class Arena:
    def __init__(self, tensor, nbytes):
        self.t = tensor
        self.nbytes = nbytes
        self.off = 0
        self.peak = 0

    def take(self, dtype, cols):
        esz = 2 if dtype == BF16 else 4
        b = cols * esz
        off = (self.off + 31) // 32 * 32
        assert off + b <= self.nbytes, f"arena overflow: {off}+{b} > {self.nbytes}"
        self.off = off + b
        self.peak = max(self.peak, self.off)
        return self.t[:, off:off + b].bitcast(dtype)

    def mark(self):
        return self.off

    def reset(self, m=0):
        self.off = m


class K:
    def __init__(self, nc, ctx):
        self.nc = nc
        self.ctx = ctx
        E = ctx.enter_context
        self.t = EP("t"); self.v = EP("v"); self.s = EP("s"); self.g = EP("g"); self.y = EP("y")
        self.mT = Sem(E(nc.semaphore("mT")))
        self.eV = Sem(E(nc.semaphore("eV")))
        self.eS = Sem(E(nc.semaphore("eS")))
        self.dG = Sem(E(nc.semaphore("dG")), 16)
        self.dY = Sem(E(nc.semaphore("dY")), 16)
        self.sems = [self.mT, self.eV, self.eS, self.dG, self.dY]
        self.prod = {id(self.mT): self.t, id(self.eV): self.v, id(self.eS): self.s,
                     id(self.dG): self.g, id(self.dY): self.y}
        self.psum = E(nc.psum_tensor("ps", [128, 4096], F32))
        self.arena_t = E(nc.sbuf_tensor("arena", [128, ARENA_BYTES], U8))
        self.A = Arena(self.arena_t, ARENA_BYTES)
        self.xbuf = E(nc.sbuf_tensor("xbuf", [128, NT * T], BF16))
        self.consts = {}
        self.inputs_used = []
        self._lastw = {}

    def tag(self, t):
        for ep in (self.t, self.v, self.s, self.g, self.y):
            ep.tag = t

    def dump(self, name, ap, cols):
        if name not in DBG_DUMP:
            return
        rows = ap.shape[0]
        y = self.nc.dram_tensor("dbg_" + name, [rows, cols], F32, kind="ExternalOutput").ap()
        for sm in (self.mT, self.eS, self.eV, self.dY):
            self.wait(self.g, sm)
        self.dma("g", y, ap)

    def pb(self, i, cols=512, dtype=F32):
        ap = self.psum[:, i * 512:(i + 1) * 512]
        if dtype != F32:
            ap = ap.bitcast(dtype)
        return ap[:, 0:cols]

    def pbs(self, i, nb):
        return self.psum[:, i * 512:(i + nb) * 512]

    def wait(self, ep, sem, n=None):
        n = sem.n if n is None else n
        if n <= 0:
            return
        key = (id(ep), id(sem))
        if self._lastw.get(key, -1) >= n:
            return
        self._lastw[key] = n
        ep(lambda e, h=sem.h, v=n: e.wait_ge(h, v))

    def barrier(self):
        for ep in (self.t, self.v, self.s, self.g, self.y):
            for sm in self.sems:
                if self.prod[id(sm)] is ep or sm.n == 0:
                    continue
                self.wait(ep, sm)

    def mm(self, out, lhsT, rhs, start, stop, inc=False):
        if inc:
            self.mT.inc()
            self.t(lambda e, o=out, l=lhsT, r=rhs, a=start, b=stop, h=self.mT.h:
                   e.matmul(o, l, r, start=a, stop=b).then_inc(h, 1))
        else:
            self.t(lambda e, o=out, l=lhsT, r=rhs, a=start, b=stop:
                   e.matmul(o, l, r, start=a, stop=b))
        return self.mT.n

    def tr(self, out, in_, ident, inc=False):
        if inc:
            self.mT.inc()
            self.t(lambda e, o=out, i=in_, d=ident, h=self.mT.h:
                   e.transpose(o, i, d).then_inc(h, 1))
        else:
            self.t(lambda e, o=out, i=in_, d=ident: e.transpose(o, i, d))
        return self.mT.n

    def ve(self, fn, inc=False):
        if inc:
            self.eV.inc()
            self.v(lambda e, f=fn, h=self.eV.h: f(e).then_inc(h, 1))
        else:
            self.v(fn)
        return self.eV.n

    def se(self, fn, inc=False):
        if inc:
            self.eS.inc()
            self.s(lambda e, f=fn, h=self.eS.h: f(e).then_inc(h, 1))
        else:
            self.s(fn)
        return self.eS.n

    def dma(self, q, out, in_):
        sem = self.dG if q == "g" else self.dY
        ep = self.g if q == "g" else self.y
        tgt = sem.inc()
        ep(lambda e, o=out, i=in_, h=sem.h: e.dma_start(out=o, in_=i).then_inc(h, 16))
        return tgt

    def din(self, name, shape, dtype):
        self.inputs_used.append(name)
        return self.nc.dram_tensor(name, list(shape), dtype, kind="ExternalInput").ap()

    def xt(self, t, c0=0, c1=T):
        return self.xbuf[:, t * T + c0:t * T + c1]


def load_consts(k, names):
    for name, shape, dt in names:
        ap = k.din(name, shape, _DT[dt])
        sb = k.ctx.enter_context(k.nc.sbuf_tensor("sb_" + name, list(shape), _DT[dt]))
        k.consts[name] = sb
        k.dma("y", sb[:], ap)




def ph_load_x_tm(k):
    k.tag('load_x')
    """Token-major x (2048,2048) bf16 -> XBUF (feature-major) via PE transposes."""
    xin = k.din("xin", (T, D), BF16)
    m0 = k.A.mark()
    stg = [k.A.take(BF16, T) for _ in range(2)]
    ident = k.consts["c_ident"]
    prev_done = [0, 0]
    for it in range(NTT):
        sl = it % 2
        if prev_done[sl]:
            k.wait(k.y, k.mT, prev_done[sl])
        tgt = k.dma("y", stg[sl][:, :], xin[it * 128:(it + 1) * 128, :])
        k.wait(k.t, k.dY, tgt)
        for dt in range(NT):
            pbank = 6 + (dt % 2)
            k.wait(k.t, k.eS)
            k.tr(k.pb(pbank, 128, BF16), stg[sl][:, dt * 128:(dt + 1) * 128], ident[:], inc=True)
            k.wait(k.s, k.mT)
            k.se(lambda e, o=k.xt(dt, it * 128, (it + 1) * 128), i=k.pb(pbank, 128, BF16):
                 e.copy(o, i), inc=True)
        prev_done[sl] = k.mT.n
    k.A.reset(m0)


def ph_store_ff(k):
    k.tag('store_ff')
    """XBUF -> handoff HBM (feature-major bf16 [2048, 2048])."""
    xh = k.nc.dram_tensor("xout", [D, T], BF16, kind="ExternalOutput").ap()
    k.barrier()
    for t in range(NT):
        k.dma("y", xh[t * 128:(t + 1) * 128, :], k.xt(t))
    k.y(lambda e, h=k.dY.h, n=k.dY.n: e.wait_ge(h, n))


def ph_load_ff(k):
    k.tag('load_ff')
    """handoff HBM -> XBUF."""
    xin = k.din("xin", (D, T), BF16)
    for t in range(NT):
        k.dma("y", k.xt(t), xin[t * 128:(t + 1) * 128, :])
    k.barrier()


def ph_out_tm(k):
    k.tag('out_tm')
    """XBUF -> token-major fp32 output via PE transposes."""
    y = k.nc.dram_tensor("xout", [T, D], F32, kind="ExternalOutput").ap()
    m0 = k.A.mark()
    ostg = [k.A.take(F32, D) for _ in range(2)]
    ident = k.consts["c_ident"]
    k.barrier()
    stg_free = [0, 0]
    for it in range(NTT):
        sl = it % 2
        for dt in range(NT):
            pbank = 6 + (dt % 2)
            k.wait(k.t, k.eS)
            k.tr(k.pb(pbank, 128, BF16), k.xt(dt, it * 128, (it + 1) * 128), ident[:], inc=True)
            k.wait(k.s, k.mT)
            if dt == 0 and stg_free[sl]:
                k.wait(k.s, k.dY, stg_free[sl])
            k.se(lambda e, o=ostg[sl][:, dt * 128:(dt + 1) * 128], i=k.pb(pbank, 128, BF16):
                 e.copy(o, i), inc=True)
        k.wait(k.y, k.eS)
        stg_free[sl] = k.dma("y", y[it * 128:(it + 1) * 128, :], ostg[sl][:, :])
    k.y(lambda e, h=k.dY.h, n=k.dY.n: e.wait_ge(h, n))
    k.A.reset(m0)


def ph_ln(k, chunks, scr):
    ones_col = k.consts["c_ones_col"]
    ones_row = k.consts["c_ones_row"]
    sq, rA, rB, rbf, bbf, rBt, bBt, tst = (scr[x] for x in ("sq", "rA", "rB", "rbf", "bbf", "rBt", "bBt", "tst"))
    k.wait(k.s, k.eV)
    k.wait(k.s, k.mT)
    for c0 in chunks:
        k.tag(f'ln c0={c0}')
        for t in range(NT):
            k.se(lambda e, o=sq[:, t * 512:(t + 1) * 512], i=k.xt(t, c0, c0 + 512):
                 e.activation(o, i, AF.Square), inc=(t == NT - 1))
        k.wait(k.t, k.eV)
        for t in range(NT):
            k.mm(k.pb(6, 512)[0:1, :], ones_col[:], k.xt(t, c0, c0 + 512),
                 start=(t == 0), stop=(t == NT - 1), inc=(t == NT - 1))
        k.wait(k.t, k.eS)
        for t in range(NT):
            k.mm(k.pb(7, 512)[0:1, :], ones_col[:], sq[:, t * 512:(t + 1) * 512],
                 start=(t == 0), stop=(t == NT - 1), inc=(t == NT - 1))
        k.wait(k.v, k.mT)
        c = 1.0 / D
        mean = rA[0:1, 0:512]; msq = rA[0:1, 512:1024]
        msq2 = rA[0:1, 1024:1536]; var = rA[0:1, 1536:2048]
        lnr = rB[0:1, 0:512]; rstd = rB[0:1, 512:1024]; brow = rB[0:1, 1024:1536]
        k.ve(lambda e, o=mean, i=k.pb(6, 512)[0:1, :], s=c: e.tensor_scalar_mul(o, i, s))
        k.ve(lambda e, o=msq, i=k.pb(7, 512)[0:1, :], s=c: e.tensor_scalar_mul(o, i, s))
        k.ve(lambda e, o=msq2, a=mean: e.tensor_mul(o, a, a))
        k.ve(lambda e, o=var, a=msq, b=msq2: e.tensor_tensor(o, a, b, op=ALU.subtract))
        k.ve(lambda e, o=var, i=var: e.tensor_scalar_add(o, i, LN_EPS), inc=True)
        k.wait(k.s, k.eV)
        k.se(lambda e, o=lnr, i=var: e.activation(o, i, AF.Ln))
        k.se(lambda e, o=rstd, i=lnr: e.activation(o, i, AF.Exp, scale=-0.5), inc=True)
        k.wait(k.v, k.eS)
        k.ve(lambda e, o=brow, a=mean, b=rstd: e.tensor_mul(o, a, b))
        k.ve(lambda e, o=brow, i=brow: e.tensor_scalar_mul(o, i, -1.0))
        k.ve(lambda e, o=rbf[0:1, :], i=rstd: e.tensor_copy(o, i))
        k.ve(lambda e, o=bbf[0:1, :], i=brow: e.tensor_copy(o, i), inc=True)
        k.wait(k.t, k.eV)
        k.mm(k.pb(4, 512), ones_row[:], rbf[0:1, :], start=True, stop=True, inc=True)
        k.mm(k.pb(5, 512), ones_row[:], bbf[0:1, :], start=True, stop=True, inc=True)
        k.wait(k.s, k.mT)
        k.se(lambda e, o=rBt[:, :], i=k.pb(4, 512): e.copy(o, i))
        k.se(lambda e, o=bBt[:, :], i=k.pb(5, 512): e.copy(o, i), inc=True)
        k.wait(k.v, k.eS)
        for t in range(NT):
            k.ve(lambda e, o=tst[:, :], a=k.xt(t, c0, c0 + 512), b=rBt[:, :]:
                 e.tensor_mul(o, a, b))
            k.ve(lambda e, o=k.xt(t, c0, c0 + 512), a=tst[:, :], b=bBt[:, :]:
                 e.tensor_add(o, a, b), inc=(t == NT - 1))


def ln_scratch(k):
    return {
        "sq": k.A.take(BF16, NT * 512),
        "rA": k.A.take(F32, 2048),
        "rB": k.A.take(F32, 1536),
        "rbf": k.A.take(BF16, 512),
        "bbf": k.A.take(BF16, 512),
        "rBt": k.A.take(BF16, 512),
        "bBt": k.A.take(BF16, 512),
        "tst": k.A.take(BF16, 512),
    }



def ph_l1ffn(k):
    nc = k.nc
    wg = k.din("w1_g", (44, 128, NT * 128), F32)
    wu = k.din("w1_u", (44, 128, NT * 128), F32)
    wd = k.din("w1_d", (16, 128, 44 * 128), F32)
    s1g = k.consts["s1_g"]
    s1u = k.consts["s1_u"]
    s1d = k.consts["s1_d"]
    m0 = k.A.mark()
    up_free = [0, 0]
    dn_free = [0, 0]
    k.barrier()
    for H in range(2):
        c0 = H * 1024
        k.A.reset(m0)
        k.wait(k.g, k.mT)
        k.wait(k.g, k.eV)
        k.wait(k.g, k.eS)
        h = k.A.take(BF16, 44 * 1024)
        gsl = [k.A.take(BF16, NT * 128) for _ in range(2)]
        usl = [k.A.take(BF16, NT * 128) for _ in range(2)]
        dsl = [k.A.take(BF16, 44 * 128) for _ in range(2)]
        sgst = k.A.take(BF16, 512)
        ust = k.A.take(BF16, 512)
        k.tag(f'l1ffn up H={H}')
        h_last_inc = 0
        for i in range(44):
            sl = i % 2
            if up_free[sl]:
                k.wait(k.g, k.mT, up_free[sl])
            tg = k.dma("g", gsl[sl][:, :], wg[i, :, :])
            tu = k.dma("g", usl[sl][:, :], wu[i, :, :])
            k.wait(k.t, k.dG, tu)
            for cc in range(2):
                cols = (c0 + cc * 512, c0 + (cc + 1) * 512)
                k.wait(k.t, k.eS)
                for t in range(NT):
                    k.mm(k.pb(0, 512), gsl[sl][:, t * 128:(t + 1) * 128],
                         k.xt(t, *cols), start=(t == 0), stop=(t == NT - 1),
                         inc=(t == NT - 1))
                for t in range(NT):
                    k.mm(k.pb(1, 512), usl[sl][:, t * 128:(t + 1) * 128],
                         k.xt(t, *cols), start=(t == 0), stop=(t == NT - 1),
                         inc=(t == NT - 1))
                up_free[sl] = k.mT.n
                k.wait(k.s, k.mT)
                k.wait(k.s, k.eV)
                k.se(lambda e, o=sgst[:, :], i=k.pb(0, 512), sc=s1g[:, i:i + 1]:
                     e.activation(o, i, AF.Silu, scale=sc))
                k.se(lambda e, o=ust[:, :], i=k.pb(1, 512), sc=s1u[:, i:i + 1]:
                     e.activation(o, i, AF.Copy, scale=sc), inc=True)
                k.wait(k.v, k.eS)
                h_last_inc = k.ve(lambda e, o=h[:, i * 1024 + cc * 512:i * 1024 + (cc + 1) * 512],
                                  a=sgst[:, :], b=ust[:, :]: e.tensor_mul(o, a, b), inc=True)
        k.tag(f'l1ffn down H={H}')
        for o in range(16):
            sl = o % 2
            if dn_free[sl]:
                k.wait(k.g, k.mT, dn_free[sl])
            td = k.dma("g", dsl[sl][:, :], wd[o, :, :])
            k.wait(k.t, k.dG, td)
            k.wait(k.t, k.eV, h_last_inc)
            for cc in range(2):
                pbank = 2 + cc
                k.wait(k.t, k.eV)
                for i in range(44):
                    k.mm(k.pb(pbank, 512), dsl[sl][:, i * 128:(i + 1) * 128],
                         h[:, i * 1024 + cc * 512:i * 1024 + (cc + 1) * 512],
                         start=(i == 0), stop=(i == 43), inc=(i == 43))
                dn_free[sl] = k.mT.n
                k.wait(k.v, k.mT)
                k.ve(lambda e, o=k.xt(o, c0 + cc * 512, c0 + (cc + 1) * 512),
                     i0=k.pb(pbank, 512), sc=s1d[:, o:o + 1],
                     i1=k.xt(o, c0 + cc * 512, c0 + (cc + 1) * 512):
                     e.scalar_tensor_tensor(o, i0, sc, i1, op0=ALU.mult, op1=ALU.add),
                     inc=True)
        k.A.reset(m0)
        scr = ln_scratch(k)
        ph_ln(k, [c0, c0 + 512], scr)
    k.A.reset(m0)



def ph_ffn0(k):
    nc = k.nc
    wsrc = []
    for s in range(12):
        wsrc.append((k.din(f"w_fg{s}", (11, 128, NT * 128), F32),
                     k.din(f"w_fu{s}", (11, 128, NT * 128), F32),
                     k.din(f"w_fd{s}", (16, 128, 11 * 128), F32)))
    m0 = k.A.mark()
    acc = k.A.take(F32, NT * 1024)
    rtb = k.A.take(BF16, 128)
    sgwb = k.A.take(BF16, 16)
    macc = k.A.mark()
    ident_f = k.consts["c_identf"]
    ones_row = k.consts["c_ones_row"]

    wrt = k.consts["w_rt"]
    wsg = k.consts["w_sg"]
    k.barrier()
    k.ve(lambda e, o=rtb[:, :], i=wrt[:, :]: e.tensor_copy(o, i))
    k.ve(lambda e, o=sgwb[:, :], i=wsg[:, :]: e.tensor_copy(o, i), inc=True)

    up_free = [0, 0]
    dn_free = [0, 0]
    for H in range(2):
        c0 = H * 1024
        k.A.reset(macc)
        k.wait(k.g, k.mT)
        k.wait(k.g, k.eV)
        k.wait(k.g, k.eS)
        h = k.A.take(BF16, 11 * 1024)
        gsl = [k.A.take(BF16, NT * 128) for _ in range(2)]
        usl = [k.A.take(BF16, NT * 128) for _ in range(2)]
        dsl = [k.A.take(BF16, 11 * 128) for _ in range(2)]
        sgst = k.A.take(BF16, 512)
        ust = k.A.take(BF16, 512)
        lgsb = k.A.take(F32, 512)
        lgtm = k.A.take(F32, 4 * 8)
        prb = k.A.take(F32, 8)
        msk = k.A.take(F32, 8)
        pm2 = k.A.take(F32, 8)
        colv = k.A.take(F32, 4)
        cmbf = k.A.take(F32, 8)
        cmbb = k.A.take(BF16, 8)
        cmbT = k.A.take(BF16, 8 * 1024)
        sgrow = k.A.take(BF16, 1024)
        cbt = [k.A.take(BF16, 512) for _ in range(2)]
        rowD = k.A.take(F32, 512)
        k.tag(f'ffn0 router H={H}')
        for cc in range(2):
            cols = (c0 + cc * 512, c0 + (cc + 1) * 512)
            k.wait(k.t, k.eV)
            for t in range(NT):
                k.mm(k.pb(5, 512)[0:8, :], rtb[:, t * 8:(t + 1) * 8], k.xt(t, *cols),
                     start=(t == 0), stop=(t == NT - 1), inc=(t == NT - 1))
            k.wait(k.v, k.mT)
            k.ve(lambda e, o=lgsb[0:8, :], i=k.pb(5, 512)[0:8, :]: e.tensor_copy(o, i),
                 inc=True)
            if cc == 0:
                k.dump(f"lgsb_H{H}", lgsb[0:8, :], 512)
            for tt in range(4):
                k.wait(k.t, k.eV)
                k.tr(k.pb(6, 8), lgsb[0:8, tt * 128:(tt + 1) * 128], ident_f[0:8, 0:8], inc=True)
                k.wait(k.v, k.mT)
                k.ve(lambda e, o=lgtm[:, tt * 8:(tt + 1) * 8], i=k.pb(6, 8):
                     e.tensor_copy(o, i), inc=True)
            for tt in range(4):
                lg = lgtm[:, tt * 8:(tt + 1) * 8]
                k.ve(lambda e, o=colv[:, 0:1], i=lg: e.reduce_max(o, i, axis=AX.X), inc=True)
                k.wait(k.v, k.eV)
                k.ve(lambda e, o=colv[:, 0:1], i=colv[:, 0:1]: e.tensor_scalar_mul(o, i, -1.0),
                     inc=True)
                k.wait(k.s, k.eV)
                k.se(lambda e, o=prb[:, :], i=lg, b=colv[:, 0:1]:
                     e.activation(o, i, AF.Exp, bias=b), inc=True)
                k.wait(k.v, k.eS)
                k.ve(lambda e, o=colv[:, 1:2], i=prb[:, :]: e.reduce_max(o, i, axis=AX.X),
                     inc=True)
                k.wait(k.v, k.eV)
                def vs(fn):
                    k.ve(fn, inc=True)
                    k.wait(k.v, k.eV)
                vs(lambda e, o=msk[:, :], i=prb[:, :], s=colv[:, 1:2]:
                   e.tensor_single_scalar(o, i, s, op=ALU.is_ge))
                vs(lambda e, o=msk[:, :], i=msk[:, :]:
                   e.tensor_scalar(o, i, -1.0, 1.0, op0=ALU.mult, op1=ALU.add))
                vs(lambda e, o=pm2[:, :], a=prb[:, :], b=msk[:, :]: e.tensor_mul(o, a, b))
                vs(lambda e, o=colv[:, 2:3], i=pm2[:, :]: e.reduce_max(o, i, axis=AX.X))
                vs(lambda e, o=msk[:, :], i=prb[:, :], s=colv[:, 2:3]:
                   e.tensor_single_scalar(o, i, s, op=ALU.is_ge))
                vs(lambda e, o=colv[:, 3:4], a=colv[:, 1:2], b=colv[:, 2:3]:
                   e.tensor_add(o, a, b))
                vs(lambda e, o=colv[:, 3:4], i=colv[:, 3:4]: e.reciprocal(o, i))
                vs(lambda e, o=cmbf[:, :], a=prb[:, :], b=msk[:, :]: e.tensor_mul(o, a, b))
                vs(lambda e, o=cmbf[:, :], i=cmbf[:, :], s=colv[:, 3:4]:
                   e.tensor_scalar_mul(o, i, s))
                k.wait(k.v, k.mT)
                k.ve(lambda e, o=cmbb[:, :], i=cmbf[:, :]: e.tensor_copy(o, i), inc=True)
                if tt == 0 and cc == 0:
                    k.dump(f"colv4_H{H}", colv[:, 0:4], 4)
                if tt == 0 and cc == 0:
                    k.dump(f"cmbf_H{H}", cmbf[:, :], 8)
                    k.dump(f"prb_H{H}", prb[:, :], 8)
                    k.dump(f"lgtm_H{H}", lgtm[:, 0:8], 8)
                k.wait(k.t, k.eV)
                for ex in range(8):
                    pbank = 6 + (ex % 2)
                    k.wait(k.t, k.eS)
                    k.tr(k.pb(pbank, 128, BF16)[0:1, :], cmbb[:, ex:ex + 1],
                         k.consts["c_ident"][:], inc=True)
                    k.wait(k.s, k.mT)
                    k.se(lambda e,
                         o=cmbT[0:1, ex * 1024 + cc * 512 + tt * 128:
                                ex * 1024 + cc * 512 + (tt + 1) * 128],
                         i=k.pb(pbank, 128, BF16)[0:1, :]: e.copy(o, i), inc=True)
            k.wait(k.t, k.eV)
            for t in range(NT):
                k.mm(k.pb(5, 512)[0:1, :], sgwb[:, t:t + 1], k.xt(t, *cols),
                     start=(t == 0), stop=(t == NT - 1), inc=(t == NT - 1))
            k.wait(k.s, k.mT)
            k.wait(k.s, k.eV)
            k.se(lambda e, o=rowD[0:1, :], i=k.pb(5, 512)[0:1, :]:
                 e.activation(o, i, AF.Exp, scale=-1.0), inc=True)
            k.wait(k.v, k.eS)
            k.ve(lambda e, o=rowD[0:1, :], i=rowD[0:1, :]: e.tensor_scalar_add(o, i, 1.0))
            k.ve(lambda e, o=rowD[0:1, :], i=rowD[0:1, :]: e.reciprocal(o, i))
            k.ve(lambda e, o=sgrow[0:1, cc * 512:(cc + 1) * 512],
                 i=rowD[0:1, :]: e.tensor_copy(o, i), inc=True)

        k.tag(f'ffn0 acc init H={H}')
        for t in range(NT):
            k.ve(lambda e, o=acc[:, t * 1024:(t + 1) * 1024], i=k.xt(t, c0, c0 + 1024):
                 e.tensor_copy(o, i), inc=(t == NT - 1))

        for s in FFN0_SOURCES:
            k.tag(f'ffn0 src {s} H={H}')
            wgd, wud, wdd = wsrc[s]
            for cc in range(2):
                k.wait(k.t, k.eV)
                k.wait(k.t, k.eS)
                if s < 4:
                    rhs = sgrow[0:1, cc * 512:(cc + 1) * 512]
                else:
                    ex = s - 4
                    rhs = cmbT[0:1, ex * 1024 + cc * 512:ex * 1024 + (cc + 1) * 512]
                k.mm(k.pb(4, 512), ones_row[:], rhs, start=True, stop=True, inc=True)
                k.wait(k.s, k.mT)
                k.se(lambda e, o=cbt[cc][:, :], i=k.pb(4, 512): e.copy(o, i), inc=True)
                if cc == 0:
                    k.dump(f"cbt{s}_H{H}", cbt[0][:, :], 512)
            h_last_inc = 0
            for i in range(11):
                sl = i % 2
                if up_free[sl]:
                    k.wait(k.g, k.mT, up_free[sl])
                k.dma("g", gsl[sl][:, :], wgd[i, :, :])
                tu = k.dma("g", usl[sl][:, :], wud[i, :, :])
                k.wait(k.t, k.dG, tu)
                for cc in range(2):
                    cols = (c0 + cc * 512, c0 + (cc + 1) * 512)
                    k.wait(k.t, k.eS)
                    for t in range(NT):
                        k.mm(k.pb(0, 512), gsl[sl][:, t * 128:(t + 1) * 128], k.xt(t, *cols),
                             start=(t == 0), stop=(t == NT - 1), inc=(t == NT - 1))
                    for t in range(NT):
                        k.mm(k.pb(1, 512), usl[sl][:, t * 128:(t + 1) * 128], k.xt(t, *cols),
                             start=(t == 0), stop=(t == NT - 1), inc=(t == NT - 1))
                    up_free[sl] = k.mT.n
                    k.wait(k.s, k.mT)
                    k.wait(k.s, k.eV)
                    k.se(lambda e, o=sgst[:, :], i=k.pb(0, 512):
                         e.activation(o, i, AF.Silu))
                    k.se(lambda e, o=ust[:, :], i=k.pb(1, 512):
                         e.activation(o, i, AF.Copy), inc=True)
                    k.wait(k.v, k.eS)
                    if FFN0_SKIP_H:
                        h_last_inc = k.ve(
                            lambda e, o=h[:, i * 1024 + cc * 512:i * 1024 + (cc + 1) * 512],
                            i=sgst[:, :]: e.tensor_copy(o, i), inc=True)
                    else:
                        k.ve(lambda e, o=ust[:, :], a=sgst[:, :], b=ust[:, :]:
                             e.tensor_mul(o, a, b))
                        h_last_inc = k.ve(
                            lambda e, o=h[:, i * 1024 + cc * 512:i * 1024 + (cc + 1) * 512],
                            a=ust[:, :], b=cbt[cc][:, :]: e.tensor_mul(o, a, b), inc=True)
            k.dump(f"h{s}_H{H}", h[:, 0:512], 512)
            for o in ([] if FFN0_SKIP_DOWN else range(16)):
                sl = o % 2
                if dn_free[sl]:
                    k.wait(k.g, k.mT, dn_free[sl])
                td = k.dma("g", dsl[sl][:, :], wdd[o, :, :])
                k.wait(k.t, k.dG, td)
                k.wait(k.t, k.eV, h_last_inc)
                for cc in range(2):
                    pbank = 2 + cc
                    k.wait(k.t, k.eV)
                    for i in range(11):
                        k.mm(k.pb(pbank, 512), dsl[sl][:, i * 128:(i + 1) * 128],
                             h[:, i * 1024 + cc * 512:i * 1024 + (cc + 1) * 512],
                             start=(i == 0), stop=(i == 10), inc=(i == 10))
                    dn_free[sl] = k.mT.n
                    k.wait(k.v, k.mT)
                    k.ve(lambda e, o=acc[:, o * 1024 + cc * 512:o * 1024 + (cc + 1) * 512],
                         a=acc[:, o * 1024 + cc * 512:o * 1024 + (cc + 1) * 512],
                         b=k.pb(pbank, 512): e.tensor_tensor(o, a, b, op=ALU.add),
                         inc=True)
        k.dump(f"acc_H{H}", acc[:, 0:512], 512)
        k.dump(f"sg_H{H}", sgrow[0:1, 0:512], 512)
        for t in range(NT):
            k.ve(lambda e, o=k.xt(t, c0, c0 + 1024), i=acc[:, t * 1024:(t + 1) * 1024]:
                 e.tensor_copy(o, i), inc=(t == NT - 1))
        k.A.reset(macc)
        scr = ln_scratch(k)
        ph_ln(k, [c0, c0 + 512], scr)
    k.A.reset(m0)



def ph_l1attn(k):
    nc = k.nc
    w1q = k.din("w1_q", (4, 128, NT * 512), F32)
    w1k = k.din("w1_k", (8, 128, NT * 256), F32)
    w1v = k.din("w1_v", (8, 128, NT * 256), F32)
    w1p = k.din("w1_p", (4, 128, NT * 512), F32)
    b1p = k.consts["b1_p"]
    o_scr = nc.dram_tensor("l1_o_scr", [D, T], BF16, kind="Internal").ap()
    ident = k.consts["c_ident"]
    bo2 = k.consts["c_blockones2"]
    ones_col = k.consts["c_ones_col"]

    m0 = k.A.mark()
    qT = k.A.take(BF16, NT * T)
    m1 = k.A.mark()
    qsl = [k.A.take(BF16, NT * 512) for _ in range(2)]
    k.barrier()
    k.tag('l1a qproj')
    qfree = [0, 0]
    for c4 in range(4):
        sl = c4 % 2
        if qfree[sl]:
            k.wait(k.g, k.eS, qfree[sl])
        tq = k.dma("g", qsl[sl][:, :], w1q[c4, :, :])
        k.wait(k.t, k.dG, tq)
        for o in range(4):
            for cc in range(4):
                pbank = 2 + (cc % 2)
                k.wait(k.t, k.eS)
                for t in range(NT):
                    k.mm(k.pb(pbank, 512),
                         qsl[sl][:, t * 512 + o * 128:t * 512 + (o + 1) * 128],
                         k.xt(t, cc * 512, (cc + 1) * 512),
                         start=(t == 0), stop=(t == NT - 1), inc=(t == NT - 1))
                k.wait(k.s, k.mT)
                k.se(lambda e, o2=qT[:, (c4 * 4 + o) * T + cc * 512:(c4 * 4 + o) * T + (cc + 1) * 512],
                     i=k.pb(pbank, 512): e.activation(o2, i, AF.Relu), inc=True)
        qfree[sl] = k.eS.n
    k.A.reset(m1)
    k.wait(k.g, k.mT)
    k.wait(k.g, k.eV)
    k.wait(k.g, k.eS)
    ktm = k.A.take(BF16, NTT * 256)
    vtm = k.A.take(BF16, NTT * 256)
    ksl = [k.A.take(BF16, NT * 256) for _ in range(2)]
    vsl = [k.A.take(BF16, NT * 256) for _ in range(2)]
    bd = k.A.take(BF16, 80)
    histg = k.A.take(BF16, 34)
    lostg = k.A.take(F32, 34)
    qstg = k.A.take(BF16, T)
    drow = k.A.take(F32, 512)
    drecb = k.A.take(BF16, 512)
    bcb = k.A.take(BF16, 512)
    ostg = [k.A.take(BF16, 512) for _ in range(2)]
    kvfree = [0, 0]
    for c in range(8):
        k.tag(f'l1a kv c={c}')
        sl = c % 2
        if kvfree[sl]:
            k.wait(k.g, k.mT, kvfree[sl])
        tk = k.dma("g", ksl[sl][:, :], w1k[c, :, :])
        tv = k.dma("g", vsl[sl][:, :], w1v[c, :, :])
        k.wait(k.t, k.dG, tv)
        for tt in range(NTT):
            pbank = tt % 2
            k.wait(k.t, k.eS)
            k.wait(k.t, k.eV)
            for t in range(NT):
                k.mm(k.pb(pbank, 256), k.xt(t, tt * 128, (tt + 1) * 128),
                     ksl[sl][:, t * 256:(t + 1) * 256],
                     start=(t == 0), stop=(t == NT - 1), inc=(t == NT - 1))
            k.wait(k.s, k.mT)
            k.se(lambda e, o=ktm[:, tt * 256:(tt + 1) * 256], i=k.pb(pbank, 256):
                 e.activation(o, i, AF.Relu), inc=True)
            k.wait(k.t, k.eS)
            for t in range(NT):
                k.mm(k.pb(pbank, 256), k.xt(t, tt * 128, (tt + 1) * 128),
                     vsl[sl][:, t * 256:(t + 1) * 256],
                     start=(t == 0), stop=(t == NT - 1), inc=(t == NT - 1))
            k.wait(k.v, k.mT)
            k.ve(lambda e, o=vtm[:, tt * 256:(tt + 1) * 256], i=k.pb(pbank, 256):
                 e.tensor_copy(o, i), inc=True)
        kvfree[sl] = k.mT.n
        for p in range(4):
            gp = c * 4 + p
            for hh in range(2):
                pbank = 4 + hh
                hcol = (2 * p + hh) * 32
                k.wait(k.t, k.eV)
                k.wait(k.t, k.eS)
                for tt in range(NTT):
                    k.mm(k.pb(pbank, 34)[0:32, 0:32],
                         ktm[:, tt * 256 + hcol:tt * 256 + hcol + 32],
                         vtm[:, tt * 256 + hcol:tt * 256 + hcol + 32],
                         start=(tt == 0), stop=False)
                for tt in range(NTT):
                    k.mm(k.pb(pbank, 34)[0:32, 32:33],
                         ktm[:, tt * 256 + hcol:tt * 256 + hcol + 32],
                         ones_col[:],
                         start=False, stop=(tt == NTT - 1), inc=(tt == NTT - 1))
            k.wait(k.v, k.mT)
            k.ve(lambda e, o=bd[0:128, 0:66]: e.memset(o, 0.0))
            for hh in range(2):
                pbank = 4 + hh
                r0 = hh * 32
                k.ve(lambda e, o=histg[0:32, 0:33], i=k.pb(pbank, 34)[0:32, 0:33]:
                     e.tensor_copy(o, i), inc=True)
                k.wait(k.v, k.eV)
                k.ve(lambda e, o=lostg[0:32, 0:33], a=k.pb(pbank, 34)[0:32, 0:33],
                     b=histg[0:32, 0:33]: e.tensor_tensor(o, a, b, op=ALU.subtract),
                     inc=True)
                k.wait(k.v, k.eV)
                k.ve(lambda e, o=bd[r0:r0 + 32, r0:r0 + 32], i=histg[0:32, 0:32]:
                     e.tensor_copy(o, i))
                k.ve(lambda e, o=bd[r0:r0 + 32, 64 + hh:65 + hh], i=histg[0:32, 32:33]:
                     e.tensor_copy(o, i))
                k.ve(lambda e, o=bd[64 + r0:96 + r0, r0:r0 + 32], i=lostg[0:32, 0:32]:
                     e.tensor_copy(o, i))
                k.ve(lambda e, o=bd[64 + r0:96 + r0, 64 + hh:65 + hh], i=lostg[0:32, 32:33]:
                     e.tensor_copy(o, i))
            tile = (gp * 64) // 128
            off = (gp * 64) % 128
            k.ve(lambda e, o=qstg[0:64, :], i=qT[off:off + 64, tile * T:(tile + 1) * T]:
                 e.tensor_copy(o, i))
            k.ve(lambda e, o=qstg[64:128, :], i=qT[off:off + 64, tile * T:(tile + 1) * T]:
                 e.tensor_copy(o, i), inc=True)
            for cc in range(4):
                k.wait(k.t, k.eV)
                k.mm(k.pb(6, 512)[0:66, :], bd[0:128, 0:66],
                     qstg[0:128, cc * 512:(cc + 1) * 512],
                     start=True, stop=True, inc=True)
                k.wait(k.v, k.mT)
                k.ve(lambda e, o=drow[64:66, :], i=k.pb(6, 512)[64:66, :]:
                     e.tensor_scalar_add(o, i, 1e-15))
                k.ve(lambda e, o=drow[64:66, :], i=drow[64:66, :]: e.reciprocal(o, i))
                k.ve(lambda e, o=drecb[0:2, :], i=drow[64:66, :]: e.tensor_copy(o, i),
                     inc=True)
                k.wait(k.t, k.eV)
                k.mm(k.pb(7, 512)[0:64, :], bo2[:], drecb[0:2, :], start=True, stop=True,
                     inc=True)
                k.wait(k.s, k.mT)
                k.se(lambda e, o=bcb[0:64, :], i=k.pb(7, 512)[0:64, :]: e.copy(o, i),
                     inc=True)
                osl = cc % 2
                k.wait(k.v, k.eS)
                k.wait(k.v, k.dY)
                k.ve(lambda e, o=ostg[osl][0:64, :], a=k.pb(6, 512)[0:64, :], b=bcb[0:64, :]:
                     e.tensor_mul(o, a, b), inc=True)
                k.wait(k.y, k.eV)
                k.dma("y", o_scr[gp * 64:(gp + 1) * 64, cc * 512:(cc + 1) * 512],
                      ostg[osl][0:64, :])
    k.tag('l1a proj')
    k.A.reset(m1)
    osl_t = [k.A.take(BF16, NT * 512) for _ in range(2)]
    psl = [k.A.take(BF16, NT * 512) for _ in range(2)]
    k.barrier()
    for cc in range(4):
        so = cc % 2
        k.wait(k.y, k.mT)
        ot = []
        for t in range(NT):
            ot.append(k.dma("y", osl_t[so][:, t * 512:(t + 1) * 512],
                            o_scr[t * 128:(t + 1) * 128, cc * 512:(cc + 1) * 512]))
        for c4 in range(4):
            sp = c4 % 2
            k.wait(k.g, k.mT)
            tp = k.dma("g", psl[sp][:, :], w1p[c4, :, :])
            k.wait(k.t, k.dG, tp)
            k.wait(k.t, k.dY, ot[-1])
            for o in range(4):
                oo = c4 * 4 + o
                pbank = (oo % 2)
                k.wait(k.t, k.eV)
                for t in range(NT):
                    k.mm(k.pb(pbank, 512),
                         psl[sp][:, t * 512 + o * 128:t * 512 + (o + 1) * 128],
                         osl_t[so][:, t * 512:(t + 1) * 512],
                         start=(t == 0), stop=(t == NT - 1), inc=(t == NT - 1))
                k.wait(k.v, k.mT)
                k.ve(lambda e, o2=k.xt(oo, cc * 512, (cc + 1) * 512), i0=k.pb(pbank, 512),
                     sc=b1p[:, oo:oo + 1], i1=k.xt(oo, cc * 512, (cc + 1) * 512):
                     e.scalar_tensor_tensor(o2, i0, sc, i1, op0=ALU.add, op1=ALU.add),
                     inc=True)
    k.A.reset(m0)
    scr = ln_scratch(k)
    ph_ln(k, [0, 512, 1024, 1536], scr)
    k.A.reset(m0)



def ph_l0attn(k):
    nc = k.nc
    wq = k.din("w_q", (16, 128, NT * 128), BF16)
    wk = k.din("w_k", (4, 128, NT * 128), BF16)
    wv = k.din("w_v", (4, 128, NT * 128), BF16)
    wo = k.din("w_o", (16, 128, NT * 128), BF16)
    bq = k.consts["b_q"]
    bk = k.consts["b_k"]
    bv = k.consts["b_v"]
    cos_d = k.din("c_cos", (128, 2048), BF16)
    sin_d = k.din("c_sin", (128, 2048), BF16)
    ident = k.consts["c_ident"]
    mask = k.consts["c_mask"]
    ones_row = k.consts["c_ones_row"]
    o_scr = nc.dram_tensor("l0_o_scr", [D, T], BF16, kind="Internal").ap()

    m0 = k.A.mark()
    cosT = k.A.take(BF16, 2048)
    sinS = k.A.take(BF16, 2048)
    k.dma("y", cosT[:, :], cos_d)
    k.dma("y", sinS[:, :], sin_d)
    PT = k.A.take(BF16, NTT * T)
    Qb = [k.A.take(BF16, T) for _ in range(2)]
    Kb = k.A.take(BF16, T)
    Vb = k.A.take(BF16, NTT * 128)
    Pst = [k.A.take(BF16, T) for _ in range(2)]
    Srow = [k.A.take(BF16, T) for _ in range(2)]
    raw = k.A.take(BF16, 512)
    rot = k.A.take(BF16, 512)
    rp1 = k.A.take(BF16, 512)
    rp2 = k.A.take(BF16, 512)
    wsl = [k.A.take(BF16, NT * 128) for _ in range(2)]
    mx = k.A.take(F32, 8)
    Ost = [k.A.take(BF16, T) for _ in range(2)]
    k.barrier()

    def rope_evac(psum_ap, bias_col, out_ap, ccols):
        k.wait(k.v, k.mT)
        k.wait(k.v, k.eS)
        k.ve(lambda e, o=raw[:, :], i=psum_ap, b=bias_col:
             e.tensor_scalar_add(o, i, b), inc=True)
        k.wait(k.s, k.eV)
        k.se(lambda e, o=rot[0:64, :], i=raw[64:128, :]: e.copy(o, i))
        k.se(lambda e, o=rot[64:128, :], i=raw[0:64, :]: e.copy(o, i), inc=True)
        k.wait(k.v, k.eS)
        k.ve(lambda e, o=rp1[:, :], a=rot[:, :], b=sinS[:, ccols[0]:ccols[1]]:
             e.tensor_mul(o, a, b))
        k.ve(lambda e, o=rp2[:, :], a=raw[:, :], b=cosT[:, ccols[0]:ccols[1]]:
             e.tensor_mul(o, a, b))
        k.ve(lambda e, o=out_ap, a=rp2[:, :], b=rp1[:, :]: e.tensor_add(o, a, b), inc=True)

    def proj_head(slab, bias_col, outbuf):
        for cc in range(4):
            cols = (cc * 512, (cc + 1) * 512)
            k.wait(k.t, k.eS)
            k.wait(k.t, k.eV)
            for t in range(NT):
                k.mm(k.pb(4, 512), slab[:, t * 128:(t + 1) * 128], k.xt(t, *cols),
                     start=(t == 0), stop=(t == NT - 1), inc=(t == NT - 1))
            rope_evac(k.pb(4, 512), bias_col, outbuf[:, cols[0]:cols[1]], cols)

    for kv in range(4):
        k.tag(f'l0a kv={kv}')
        slk = wsl[0]
        k.wait(k.g, k.mT)
        tk = k.dma("g", slk[:, :], wk[kv, :, :])
        k.wait(k.t, k.dG, tk)
        proj_head(slk, bk[:, kv:kv + 1], Kb)
        slv = wsl[1]
        k.wait(k.g, k.mT)
        tv = k.dma("g", slv[:, :], wv[kv, :, :])
        k.wait(k.t, k.dG, tv)
        for tt in range(NTT):
            k.wait(k.t, k.eV)
            for t in range(NT):
                k.mm(k.pb(5, 128), k.xt(t, tt * 128, (tt + 1) * 128),
                     slv[:, t * 128:(t + 1) * 128],
                     start=(t == 0), stop=False)
            k.mm(k.pb(5, 128), ones_row[:], bv[0:1, kv * 128:(kv + 1) * 128],
                 start=False, stop=True, inc=True)
            k.wait(k.v, k.mT)
            k.ve(lambda e, o=Vb[:, tt * 128:(tt + 1) * 128], i=k.pb(5, 128):
                 e.tensor_copy(o, i), inc=True)
        for g in range(4):
            hidx = kv * 4 + g
            qb = Qb[g % 2]
            slq = wsl[(g % 2)]
            k.wait(k.g, k.mT)
            tq = k.dma("g", slq[:, :], wq[hidx, :, :])
            k.wait(k.t, k.dG, tq)
            proj_head(slq, bq[:, hidx:hidx + 1], qb)
            scopy_val = [0] * NTT
            pscale_val = [0] * NTT
            trevac_val = {6: 0, 7: 0}
            trevac_valv = {6: 0, 7: 0}
            prev_tr = 0

            def emit_transposes(i):
                nonlocal prev_tr
                psl = Pst[i % 2]
                k.wait(k.t, k.eS, pscale_val[i])
                for jt in range(i + 1):
                    pbank = 6 + (jt % 2)
                    sem, vals = (k.eS, trevac_val) if jt % 2 == 0 else (k.eV, trevac_valv)
                    k.wait(k.t, sem, vals[pbank])
                    k.tr(k.pb(pbank, 128, BF16), psl[:, jt * 128:(jt + 1) * 128],
                         ident[:], inc=True)
                    tr_inc = k.mT.n
                    if jt % 2 == 0:
                        k.wait(k.s, k.mT, tr_inc)
                        k.se(lambda e, o=PT[:, jt * T + i * 128:jt * T + (i + 1) * 128],
                             i=k.pb(pbank, 128, BF16): e.copy(o, i), inc=True)
                        trevac_val[pbank] = k.eS.n
                    else:
                        k.wait(k.v, k.mT, tr_inc)
                        k.ve(lambda e, o=PT[:, jt * T + i * 128:jt * T + (i + 1) * 128],
                             i=k.pb(pbank, 128, BF16): e.tensor_copy(o, i), inc=True)
                        trevac_valv[pbank] = k.eV.n
                prev_tr = k.mT.n

            for i in range(NTT):
                L = (i + 1) * 128
                mxs = (i % 2) * 4
                srow = Srow[i % 2]
                if i >= 1:
                    k.wait(k.t, k.eV, scopy_val[i - 1])
                nsub = (L + 511) // 512
                for csub in range(nsub):
                    a = csub * 512
                    b = min(a + 512, L)
                    k.mm(k.pbs(0, 4)[:, a:b], qb[:, i * 128:(i + 1) * 128], Kb[:, a:b],
                         start=True, stop=True, inc=(csub == nsub - 1))
                sc_inc = k.mT.n
                if i >= 1:
                    emit_transposes(i - 1)
                k.wait(k.v, k.mT, sc_inc)
                k.ve(lambda e, o=srow[:, 0:L], i=k.pbs(0, 4)[:, 0:L]: e.tensor_copy(o, i),
                     inc=True)
                scopy_val[i] = k.eV.n
                k.ve(lambda e, o=srow[:, i * 128:L], a=srow[:, i * 128:L],
                     b=mask[:, :]: e.tensor_tensor(o, a, b, op=ALU.add))
                k.ve(lambda e, o=mx[:, mxs:mxs + 1], i=srow[:, 0:L]:
                     e.reduce_max(o, i, axis=AX.X), inc=True)
                k.wait(k.v, k.eV)
                k.ve(lambda e, o=mx[:, mxs + 1:mxs + 2], i=mx[:, mxs:mxs + 1]:
                     e.tensor_scalar_mul(o, i, -SCALE_L0), inc=True)
                neg_inc = k.eV.n
                k.wait(k.s, k.eV, neg_inc)
                psl = Pst[i % 2]
                k.se(lambda e, o=psl[:, 0:L], i=srow[:, 0:L], b=mx[:, mxs + 1:mxs + 2],
                     acc=mx[:, mxs + 2:mxs + 3]:
                     e.activation(o, i, AF.Exp, bias=b, scale=SCALE_L0, accum_out=acc),
                     inc=True)
                exp_inc = k.eS.n
                k.wait(k.v, k.eS, exp_inc)
                k.ve(lambda e, o=mx[:, mxs + 3:mxs + 4], i=mx[:, mxs + 2:mxs + 3]:
                     e.reciprocal(o, i), inc=True)
                rcp_inc = k.eV.n
                k.wait(k.s, k.eV, rcp_inc)
                k.se(lambda e, o=psl[:, 0:L], i=psl[:, 0:L], s=mx[:, mxs + 3:mxs + 4]:
                     e.activation(o, i, AF.Copy, scale=s), inc=True)
                pscale_val[i] = k.eS.n
            emit_transposes(NTT - 1)
            ost = Ost[g % 2]
            k.wait(k.t, k.eS)
            for ccb in range(4):
                last_jt = 4 * ccb + 3
                for jt in range(last_jt + 1):
                    off = max(0, (jt - 4 * ccb)) * 128
                    k.mm(k.pbs(0, 4)[:, ccb * 512 + off:(ccb + 1) * 512],
                         Vb[:, jt * 128:(jt + 1) * 128],
                         PT[:, jt * T + ccb * 512 + off:jt * T + (ccb + 1) * 512],
                         start=(jt == 0), stop=(jt == last_jt), inc=(jt == last_jt))
                k.wait(k.s, k.mT)
                if ccb == 0:
                    k.wait(k.s, k.dY)
                k.se(lambda e, o=ost[:, ccb * 512:(ccb + 1) * 512],
                     i=k.pbs(0, 4)[:, ccb * 512:(ccb + 1) * 512]: e.copy(o, i),
                     inc=True)
            k.wait(k.y, k.eS)
            k.dma("y", o_scr[hidx * 128:(hidx + 1) * 128, :], ost[:, :])
    k.tag('l0a wo')
    k.A.reset(m0)
    oslt = [k.A.take(BF16, NT * 512) for _ in range(2)]
    wosl = [k.A.take(BF16, NT * 128) for _ in range(2)]
    wfree = [0, 0]
    k.barrier()
    for cc in range(4):
        so = cc % 2
        k.wait(k.y, k.mT)
        ot = []
        for t in range(NT):
            ot.append(k.dma("y", oslt[so][:, t * 512:(t + 1) * 512],
                            o_scr[t * 128:(t + 1) * 128, cc * 512:(cc + 1) * 512]))
        for o in range(16):
            sw = o % 2
            if wfree[sw]:
                k.wait(k.g, k.mT, wfree[sw])
            tw = k.dma("g", wosl[sw][:, :], wo[o, :, :])
            k.wait(k.t, k.dG, tw)
            k.wait(k.t, k.dY, ot[-1])
            pbank = 4 + (o % 2)
            k.wait(k.t, k.eV)
            for t in range(NT):
                k.mm(k.pb(pbank, 512), wosl[sw][:, t * 128:(t + 1) * 128],
                     oslt[so][:, t * 512:(t + 1) * 512],
                     start=(t == 0), stop=(t == NT - 1), inc=(t == NT - 1))
            wfree[sw] = k.mT.n
            k.wait(k.v, k.mT)
            k.ve(lambda e, o2=k.xt(o, cc * 512, (cc + 1) * 512),
                 a=k.xt(o, cc * 512, (cc + 1) * 512), b=k.pb(pbank, 512):
                 e.tensor_tensor(o2, a, b, op=ALU.add), inc=True)
    k.A.reset(m0)
    scr = ln_scratch(k)
    ph_ln(k, [0, 512, 1024, 1536], scr)
    k.A.reset(m0)



BLOCK_CONSTS = {"l0attn": L0_CONSTS, "ffn0": FFN0_CONSTS, "l1attn": L1A_CONSTS,
                "l1ffn": L1F_CONSTS}
BLOCK_FN = {"l0attn": ph_l0attn, "ffn0": ph_ffn0, "l1attn": ph_l1attn, "l1ffn": ph_l1ffn}


def build_core_program(blocks, in_mode, out_mode):
    nc = bass.Bass("TRN2", target_bir_lowering=False, debug=False)
    ctx = ExitStack()
    k = K(nc, ctx)
    cl = list(CONSTS)
    for b in blocks:
        cl += BLOCK_CONSTS[b]
    load_consts(k, cl)
    if in_mode == "tm":
        ph_load_x_tm(k)
    else:
        ph_load_ff(k)
    k.barrier()
    for b in blocks:
        BLOCK_FN[b](k)
        k.barrier()
    if out_mode == "tm":
        ph_out_tm(k)
    else:
        ph_store_ff(k)
    emit_replay(nc, k)
    ctx.close()
    return nc, list(dict.fromkeys(k.inputs_used))


def emit_replay(nc, k):
    block = k.ctx.enter_context(nc.Block())
    block.tensor(lambda e: k.t.replay(e))
    block.vector(lambda e: k.v.replay(e))
    block.scalar(lambda e: k.s.replay(e))
    block.gpsimd(lambda e: k.g.replay(e))
    block.sync(lambda e: k.y.replay(e))


ALL_BLOCKS = ["l0attn", "ffn0", "l1attn", "l1ffn"]


def kernel(**inputs):
    pk = host_pack(inputs)
    x = np.asarray(inputs["L0_attn_x"])
    cur = [np.ascontiguousarray(x[c]).astype(BF, copy=False) for c in range(8)]
    nc, names = build_core_program(ALL_BLOCKS, "tm", "tm")
    in_maps = []
    for c in range(8):
        m = {}
        for n in names:
            m[n] = cur[c] if n == "xin" else pk[n]
        in_maps.append(m)
    res = run_bass_kernel_spmd(nc, in_maps, core_ids=list(range(8)))
    return np.stack([np.asarray(res.results[c]["xout"], dtype=np.float32) for c in range(8)])


# version 09
# speedup vs baseline: 1.0485x; 1.0485x over previous
import numpy as np
import ml_dtypes
from contextlib import ExitStack

import concourse.bass as bass
import concourse.mybir as mybir
from concourse.bass_utils import run_bass_kernel_spmd

F32 = mybir.dt.float32
BF16 = mybir.dt.bfloat16
U8 = mybir.dt.uint8
AF = mybir.ActivationFunctionType
ALU = mybir.AluOpType
AX = mybir.AxisListType
BF = ml_dtypes.bfloat16

T = 2048
D = 2048
NT = D // 128
NTT = T // 128
NEG_HUGE = float(np.finfo(np.float32).min) / 4.0
LN_EPS = 1e-5
SCALE_L0 = 1.0 / float(np.sqrt(128.0))
ARENA_BYTES = 136 * 1024
FFN0_SOURCES = list(range(12))
FFN0_SKIP_DOWN = False
FFN0_SKIP_H = False
DBG_DUMP = set()



def _bf(a):
    return np.ascontiguousarray(np.asarray(a).astype(BF))


def _f32(a):
    return np.ascontiguousarray(np.asarray(a, dtype=np.float32))


def pack_dmajor(wT, chunk):
    Din, OF = wT.shape
    nt = Din // 128
    assert OF % chunk == 0
    nch = OF // chunk
    a = wT.reshape(nt, 128, nch, chunk).transpose(2, 1, 0, 3).reshape(nch, 128, nt * chunk)
    return np.ascontiguousarray(a)


def pack_vec_ptiles(v):
    n = v.shape[0] // 128
    return np.ascontiguousarray(v.reshape(n, 128).T)


def rope_tables():
    half = 64
    inv = 1.0 / (1000000.0 ** (np.arange(half, dtype=np.float64) / half))
    ang = np.outer(inv, np.arange(T, dtype=np.float64))
    cos = np.cos(ang).astype(np.float32)
    sin = np.sin(ang).astype(np.float32)
    cosT = np.concatenate([cos, cos], axis=0)
    sinS = np.concatenate([-sin, sin], axis=0)
    return np.ascontiguousarray(cosT), np.ascontiguousarray(sinS)


CONSTS = [
    ("c_ident", (128, 128), "bf16"),
    ("c_identf", (128, 128), "f32"),
    ("c_mask", (128, 128), "f32"),
    ("c_ones_row", (1, 128), "bf16"),
    ("c_ones_col", (128, 1), "bf16"),
    ("c_blockones2", (2, 64), "bf16"),
]
L0_CONSTS = [
    ("b_q", (128, 16), "f32"),
    ("b_k", (128, 4), "f32"),
    ("b_v", (1, 512), "bf16"),
]
FFN0_CONSTS = [
    ("w_rt", (128, 128), "f32"),
    ("w_sg", (128, 16), "f32"),
]
L1A_CONSTS = [
    ("b1_p", (128, 16), "f32"),
]
L1F_CONSTS = [
    ("s1_g", (128, 44), "f32"),
    ("s1_u", (128, 44), "f32"),
    ("s1_d", (128, 16), "f32"),
]


def host_pack(inputs):
    pk = {}
    pk["c_ident"] = np.eye(128, dtype=BF)
    pk["c_identf"] = np.eye(128, dtype=np.float32)
    m = np.zeros((128, 128), np.float32)
    m[np.triu_indices(128, 1)] = NEG_HUGE
    pk["c_mask"] = m
    pk["c_ones_row"] = np.ones((1, 128), BF)
    pk["c_ones_col"] = np.ones((128, 1), BF)
    b2 = np.zeros((2, 64), BF)
    b2[0, 0:32] = 1
    b2[1, 32:64] = 1
    pk["c_blockones2"] = b2
    cosT, sinS = rope_tables()
    pk["c_cos"] = _bf(cosT)
    pk["c_sin"] = _bf(sinS)

    pk["w_q"] = pack_dmajor(_bf(inputs["L0_attn_Wq"]), 128)
    pk["w_k"] = pack_dmajor(_bf(inputs["L0_attn_Wk"]), 128)
    pk["w_v"] = pack_dmajor(_bf(inputs["L0_attn_Wv"]), 128)
    pk["w_o"] = pack_dmajor(_bf(inputs["L0_attn_Wo"]), 128)
    pk["b_q"] = pack_vec_ptiles(_f32(inputs["L0_attn_bq"]))
    pk["b_k"] = pack_vec_ptiles(_f32(inputs["L0_attn_bk"]))
    pk["b_v"] = _bf(_f32(inputs["L0_attn_bv"]).reshape(1, 512))

    rw = _f32(inputs["L0_ffn_router_weight"])
    pk["w_rt"] = np.ascontiguousarray(pack_dmajor(np.ascontiguousarray(rw.T), 8)[0])
    gu = _f32(inputs["L0_ffn_gate_up_proj"])
    dn = _f32(inputs["L0_ffn_down_proj"])
    for e in range(8):
        guT = np.ascontiguousarray(gu[e].T)
        gp = pack_dmajor(np.ascontiguousarray(guT[:, :1408]), 128)
        up = pack_dmajor(np.ascontiguousarray(guT[:, 1408:]), 128)
        pk[f"w_fgu{e + 4}"] = np.ascontiguousarray(np.concatenate([gp, up], axis=2))
        pk[f"w_fd{e + 4}"] = pack_dmajor(np.ascontiguousarray(dn[e].T), 128)
    shgT = np.ascontiguousarray(_f32(inputs["L0_ffn_sh_gate_w"]).T)
    shuT = np.ascontiguousarray(_f32(inputs["L0_ffn_sh_up_w"]).T)
    shdT = np.ascontiguousarray(_f32(inputs["L0_ffn_sh_down_w"]).T)
    for s in range(4):
        gp = pack_dmajor(np.ascontiguousarray(shgT[:, s * 1408:(s + 1) * 1408]), 128)
        up = pack_dmajor(np.ascontiguousarray(shuT[:, s * 1408:(s + 1) * 1408]), 128)
        pk[f"w_fgu{s}"] = np.ascontiguousarray(np.concatenate([gp, up], axis=2))
        pk[f"w_fd{s}"] = pack_dmajor(np.ascontiguousarray(shdT[s * 1408:(s + 1) * 1408, :]), 128)
    pk["w_sg"] = pack_vec_ptiles(_f32(inputs["L0_ffn_shared_gate_w"]).reshape(-1))

    wqkvT = np.ascontiguousarray(_f32(inputs["L1_attn_w_qkv"]).T)
    pk["w1_q"] = pack_dmajor(np.ascontiguousarray(wqkvT[:, 0:2048]), 512)
    pk["w1_k"] = pack_dmajor(np.ascontiguousarray(wqkvT[:, 2048:4096]), 256)
    pk["w1_v"] = pack_dmajor(np.ascontiguousarray(wqkvT[:, 4096:6144]), 256)
    pk["w1_p"] = pack_dmajor(np.ascontiguousarray(_f32(inputs["L1_attn_w_proj"]).T), 512)
    pk["b1_p"] = pack_vec_ptiles(_f32(inputs["L1_attn_b_proj"]))

    g1 = pack_dmajor(np.ascontiguousarray(_f32(inputs["L1_ffn_gate_w"]).T), 128)
    u1 = pack_dmajor(np.ascontiguousarray(_f32(inputs["L1_ffn_up_w"]).T), 128)
    pk["w1_gu"] = np.ascontiguousarray(np.concatenate([g1, u1], axis=2))
    pk["w1_d"] = pack_dmajor(np.ascontiguousarray(_f32(inputs["L1_ffn_down_w"]).T), 128)
    pk["s1_g"] = pack_vec_ptiles(_f32(inputs["L1_ffn_gate_s"]))
    pk["s1_u"] = pack_vec_ptiles(_f32(inputs["L1_ffn_up_s"]))
    pk["s1_d"] = pack_vec_ptiles(_f32(inputs["L1_ffn_down_s"]))
    return pk



_DT = {"f32": F32, "bf16": BF16}


class Sem:
    def __init__(self, h, step=1):
        self.h = h
        self.n = 0
        self.step = step

    def inc(self):
        self.n += self.step
        return self.n


class EP:
    def __init__(self, name=""):
        self.name = name
        self.ops = []
        self.tag = ""

    def __call__(self, fn):
        self.ops.append((fn, self.tag))

    def replay(self, eng):
        for idx, (fn, tag) in enumerate(self.ops):
            try:
                fn(eng)
            except Exception as ex:
                dd = []
                for dv in (fn.__defaults__ or ()):
                    try:
                        dd.append(getattr(dv, "concise", lambda: repr(dv))())
                    except Exception:
                        dd.append(str(type(dv)))
                raise RuntimeError(
                    f"engine {self.name} op {idx} tag={tag}: {ex}\nargs={dd}") from ex


class Arena:
    def __init__(self, tensor, nbytes):
        self.t = tensor
        self.nbytes = nbytes
        self.off = 0
        self.peak = 0

    def take(self, dtype, cols):
        esz = 2 if dtype == BF16 else 4
        b = cols * esz
        off = (self.off + 31) // 32 * 32
        assert off + b <= self.nbytes, f"arena overflow: {off}+{b} > {self.nbytes}"
        self.off = off + b
        self.peak = max(self.peak, self.off)
        return self.t[:, off:off + b].bitcast(dtype)

    def mark(self):
        return self.off

    def reset(self, m=0):
        self.off = m


class K:
    def __init__(self, nc, ctx):
        self.nc = nc
        self.ctx = ctx
        E = ctx.enter_context
        self.t = EP("t"); self.v = EP("v"); self.s = EP("s"); self.g = EP("g"); self.y = EP("y")
        self.mT = Sem(E(nc.semaphore("mT")))
        self.eV = Sem(E(nc.semaphore("eV")))
        self.eS = Sem(E(nc.semaphore("eS")))
        self.dG = Sem(E(nc.semaphore("dG")), 16)
        self.dY = Sem(E(nc.semaphore("dY")), 16)
        self.sems = [self.mT, self.eV, self.eS, self.dG, self.dY]
        self.prod = {id(self.mT): self.t, id(self.eV): self.v, id(self.eS): self.s,
                     id(self.dG): self.g, id(self.dY): self.y}
        self.psum = E(nc.psum_tensor("ps", [128, 4096], F32))
        self.arena_t = E(nc.sbuf_tensor("arena", [128, ARENA_BYTES], U8))
        self.A = Arena(self.arena_t, ARENA_BYTES)
        self.xbuf = E(nc.sbuf_tensor("xbuf", [128, NT * T], BF16))
        self.consts = {}
        self.inputs_used = []
        self._lastw = {}

    def tag(self, t):
        for ep in (self.t, self.v, self.s, self.g, self.y):
            ep.tag = t

    def dump(self, name, ap, cols):
        if name not in DBG_DUMP:
            return
        rows = ap.shape[0]
        y = self.nc.dram_tensor("dbg_" + name, [rows, cols], F32, kind="ExternalOutput").ap()
        for sm in (self.mT, self.eS, self.eV, self.dY):
            self.wait(self.g, sm)
        self.dma("g", y, ap)

    def pb(self, i, cols=512, dtype=F32):
        ap = self.psum[:, i * 512:(i + 1) * 512]
        if dtype != F32:
            ap = ap.bitcast(dtype)
        return ap[:, 0:cols]

    def pbs(self, i, nb):
        return self.psum[:, i * 512:(i + nb) * 512]

    def wait(self, ep, sem, n=None):
        n = sem.n if n is None else n
        if n <= 0:
            return
        key = (id(ep), id(sem))
        if self._lastw.get(key, -1) >= n:
            return
        self._lastw[key] = n
        ep(lambda e, h=sem.h, v=n: e.wait_ge(h, v))

    def barrier(self):
        for ep in (self.t, self.v, self.s, self.g, self.y):
            for sm in self.sems:
                if self.prod[id(sm)] is ep or sm.n == 0:
                    continue
                self.wait(ep, sm)

    def mm(self, out, lhsT, rhs, start, stop, inc=False):
        if inc:
            self.mT.inc()
            self.t(lambda e, o=out, l=lhsT, r=rhs, a=start, b=stop, h=self.mT.h:
                   e.matmul(o, l, r, start=a, stop=b).then_inc(h, 1))
        else:
            self.t(lambda e, o=out, l=lhsT, r=rhs, a=start, b=stop:
                   e.matmul(o, l, r, start=a, stop=b))
        return self.mT.n

    def tr(self, out, in_, ident, inc=False):
        if inc:
            self.mT.inc()
            self.t(lambda e, o=out, i=in_, d=ident, h=self.mT.h:
                   e.transpose(o, i, d).then_inc(h, 1))
        else:
            self.t(lambda e, o=out, i=in_, d=ident: e.transpose(o, i, d))
        return self.mT.n

    def ve(self, fn, inc=False):
        if inc:
            self.eV.inc()
            self.v(lambda e, f=fn, h=self.eV.h: f(e).then_inc(h, 1))
        else:
            self.v(fn)
        return self.eV.n

    def se(self, fn, inc=False):
        if inc:
            self.eS.inc()
            self.s(lambda e, f=fn, h=self.eS.h: f(e).then_inc(h, 1))
        else:
            self.s(fn)
        return self.eS.n

    def dma(self, q, out, in_):
        sem = self.dG if q == "g" else self.dY
        ep = self.g if q == "g" else self.y
        tgt = sem.inc()
        ep(lambda e, o=out, i=in_, h=sem.h: e.dma_start(out=o, in_=i).then_inc(h, 16))
        return tgt

    def din(self, name, shape, dtype):
        self.inputs_used.append(name)
        return self.nc.dram_tensor(name, list(shape), dtype, kind="ExternalInput").ap()

    def xt(self, t, c0=0, c1=T):
        return self.xbuf[:, t * T + c0:t * T + c1]


def load_consts(k, names):
    for name, shape, dt in names:
        ap = k.din(name, shape, _DT[dt])
        sb = k.ctx.enter_context(k.nc.sbuf_tensor("sb_" + name, list(shape), _DT[dt]))
        k.consts[name] = sb
        k.dma("y", sb[:], ap)




def ph_load_x_tm(k):
    k.tag('load_x')
    """Token-major x (2048,2048) bf16 -> XBUF (feature-major) via PE transposes."""
    xin = k.din("xin", (T, D), BF16)
    m0 = k.A.mark()
    stg = [k.A.take(BF16, T) for _ in range(2)]
    ident = k.consts["c_ident"]
    prev_done = [0, 0]
    for it in range(NTT):
        sl = it % 2
        if prev_done[sl]:
            k.wait(k.y, k.mT, prev_done[sl])
        tgt = k.dma("y", stg[sl][:, :], xin[it * 128:(it + 1) * 128, :])
        k.wait(k.t, k.dY, tgt)
        for dt in range(NT):
            pbank = 6 + (dt % 2)
            k.wait(k.t, k.eS)
            k.tr(k.pb(pbank, 128, BF16), stg[sl][:, dt * 128:(dt + 1) * 128], ident[:], inc=True)
            k.wait(k.s, k.mT)
            k.se(lambda e, o=k.xt(dt, it * 128, (it + 1) * 128), i=k.pb(pbank, 128, BF16):
                 e.copy(o, i), inc=True)
        prev_done[sl] = k.mT.n
    k.A.reset(m0)


def ph_store_ff(k):
    k.tag('store_ff')
    """XBUF -> handoff HBM (feature-major bf16 [2048, 2048])."""
    xh = k.nc.dram_tensor("xout", [D, T], BF16, kind="ExternalOutput").ap()
    k.barrier()
    for t in range(NT):
        k.dma("y", xh[t * 128:(t + 1) * 128, :], k.xt(t))
    k.y(lambda e, h=k.dY.h, n=k.dY.n: e.wait_ge(h, n))


def ph_load_ff(k):
    k.tag('load_ff')
    """handoff HBM -> XBUF."""
    xin = k.din("xin", (D, T), BF16)
    for t in range(NT):
        k.dma("y", k.xt(t), xin[t * 128:(t + 1) * 128, :])
    k.barrier()


def ph_out_tm(k):
    k.tag('out_tm')
    """XBUF -> token-major fp32 output via PE transposes."""
    y = k.nc.dram_tensor("xout", [T, D], F32, kind="ExternalOutput").ap()
    m0 = k.A.mark()
    ostg = [k.A.take(F32, D) for _ in range(2)]
    ident = k.consts["c_ident"]
    k.barrier()
    stg_free = [0, 0]
    for it in range(NTT):
        sl = it % 2
        for dt in range(NT):
            pbank = 6 + (dt % 2)
            k.wait(k.t, k.eS)
            k.tr(k.pb(pbank, 128, BF16), k.xt(dt, it * 128, (it + 1) * 128), ident[:], inc=True)
            k.wait(k.s, k.mT)
            if dt == 0 and stg_free[sl]:
                k.wait(k.s, k.dY, stg_free[sl])
            k.se(lambda e, o=ostg[sl][:, dt * 128:(dt + 1) * 128], i=k.pb(pbank, 128, BF16):
                 e.copy(o, i), inc=True)
        k.wait(k.y, k.eS)
        stg_free[sl] = k.dma("y", y[it * 128:(it + 1) * 128, :], ostg[sl][:, :])
    k.y(lambda e, h=k.dY.h, n=k.dY.n: e.wait_ge(h, n))
    k.A.reset(m0)


def ph_ln(k, chunks, scr):
    ones_col = k.consts["c_ones_col"]
    ones_row = k.consts["c_ones_row"]
    sq, rA, rB, rbf, bbf, rBt, bBt, tst = (scr[x] for x in ("sq", "rA", "rB", "rbf", "bbf", "rBt", "bBt", "tst"))
    k.wait(k.s, k.eV)
    k.wait(k.s, k.mT)
    for c0 in chunks:
        k.tag(f'ln c0={c0}')
        for t in range(NT):
            k.se(lambda e, o=sq[:, t * 512:(t + 1) * 512], i=k.xt(t, c0, c0 + 512):
                 e.activation(o, i, AF.Square), inc=(t == NT - 1))
        k.wait(k.t, k.eV)
        for t in range(NT):
            k.mm(k.pb(6, 512)[0:1, :], ones_col[:], k.xt(t, c0, c0 + 512),
                 start=(t == 0), stop=(t == NT - 1), inc=(t == NT - 1))
        k.wait(k.t, k.eS)
        for t in range(NT):
            k.mm(k.pb(7, 512)[0:1, :], ones_col[:], sq[:, t * 512:(t + 1) * 512],
                 start=(t == 0), stop=(t == NT - 1), inc=(t == NT - 1))
        k.wait(k.v, k.mT)
        c = 1.0 / D
        mean = rA[0:1, 0:512]; msq = rA[0:1, 512:1024]
        msq2 = rA[0:1, 1024:1536]; var = rA[0:1, 1536:2048]
        lnr = rB[0:1, 0:512]; rstd = rB[0:1, 512:1024]; brow = rB[0:1, 1024:1536]
        k.ve(lambda e, o=mean, i=k.pb(6, 512)[0:1, :], s=c: e.tensor_scalar_mul(o, i, s))
        k.ve(lambda e, o=msq, i=k.pb(7, 512)[0:1, :], s=c: e.tensor_scalar_mul(o, i, s))
        k.ve(lambda e, o=msq2, a=mean: e.tensor_mul(o, a, a))
        k.ve(lambda e, o=var, a=msq, b=msq2: e.tensor_tensor(o, a, b, op=ALU.subtract))
        k.ve(lambda e, o=var, i=var: e.tensor_scalar_add(o, i, LN_EPS), inc=True)
        k.wait(k.s, k.eV)
        k.se(lambda e, o=lnr, i=var: e.activation(o, i, AF.Ln))
        k.se(lambda e, o=rstd, i=lnr: e.activation(o, i, AF.Exp, scale=-0.5), inc=True)
        k.wait(k.v, k.eS)
        k.ve(lambda e, o=brow, a=mean, b=rstd: e.tensor_mul(o, a, b))
        k.ve(lambda e, o=brow, i=brow: e.tensor_scalar_mul(o, i, -1.0))
        k.ve(lambda e, o=rbf[0:1, :], i=rstd: e.tensor_copy(o, i))
        k.ve(lambda e, o=bbf[0:1, :], i=brow: e.tensor_copy(o, i), inc=True)
        k.wait(k.t, k.eV)
        k.mm(k.pb(4, 512), ones_row[:], rbf[0:1, :], start=True, stop=True, inc=True)
        k.mm(k.pb(5, 512), ones_row[:], bbf[0:1, :], start=True, stop=True, inc=True)
        k.wait(k.s, k.mT)
        k.se(lambda e, o=rBt[:, :], i=k.pb(4, 512): e.copy(o, i))
        k.se(lambda e, o=bBt[:, :], i=k.pb(5, 512): e.copy(o, i), inc=True)
        k.wait(k.v, k.eS)
        for t in range(NT):
            k.ve(lambda e, o=tst[:, :], a=k.xt(t, c0, c0 + 512), b=rBt[:, :]:
                 e.tensor_mul(o, a, b))
            k.ve(lambda e, o=k.xt(t, c0, c0 + 512), a=tst[:, :], b=bBt[:, :]:
                 e.tensor_add(o, a, b), inc=(t == NT - 1))


def ln_scratch(k):
    return {
        "sq": k.A.take(BF16, NT * 512),
        "rA": k.A.take(F32, 2048),
        "rB": k.A.take(F32, 1536),
        "rbf": k.A.take(BF16, 512),
        "bbf": k.A.take(BF16, 512),
        "rBt": k.A.take(BF16, 512),
        "bBt": k.A.take(BF16, 512),
        "tst": k.A.take(BF16, 512),
    }



def ph_l1ffn(k):
    nc = k.nc
    wgu = k.din("w1_gu", (44, 128, 2 * NT * 128), F32)
    wd = k.din("w1_d", (16, 128, 44 * 128), F32)
    s1g = k.consts["s1_g"]
    s1u = k.consts["s1_u"]
    s1d = k.consts["s1_d"]
    m0 = k.A.mark()
    up_free = [0, 0]
    dn_free = [0, 0]
    k.barrier()
    for H in range(2):
        c0 = H * 1024
        k.A.reset(m0)
        k.wait(k.g, k.mT)
        k.wait(k.g, k.eV)
        k.wait(k.g, k.eS)
        h = k.A.take(BF16, 44 * 1024)
        gusl = [k.A.take(BF16, 2 * NT * 128) for _ in range(2)]
        dsl = [k.A.take(BF16, 44 * 128) for _ in range(2)]
        sgst = k.A.take(BF16, 512)
        ust = k.A.take(BF16, 512)
        k.tag(f'l1ffn up H={H}')
        h_last_inc = 0
        for i in range(44):
            sl = i % 2
            if up_free[sl]:
                k.wait(k.g, k.mT, up_free[sl])
            tu = k.dma("g", gusl[sl][:, :], wgu[i, :, :])
            k.wait(k.t, k.dG, tu)
            goff = 0
            uoff = NT * 128
            for cc in range(2):
                cols = (c0 + cc * 512, c0 + (cc + 1) * 512)
                k.wait(k.t, k.eS)
                for t in range(NT):
                    k.mm(k.pb(0, 512), gusl[sl][:, goff + t * 128:goff + (t + 1) * 128],
                         k.xt(t, *cols), start=(t == 0), stop=(t == NT - 1),
                         inc=(t == NT - 1))
                for t in range(NT):
                    k.mm(k.pb(1, 512), gusl[sl][:, uoff + t * 128:uoff + (t + 1) * 128],
                         k.xt(t, *cols), start=(t == 0), stop=(t == NT - 1),
                         inc=(t == NT - 1))
                up_free[sl] = k.mT.n
                k.wait(k.s, k.mT)
                k.wait(k.s, k.eV)
                k.se(lambda e, o=sgst[:, :], i=k.pb(0, 512), sc=s1g[:, i:i + 1]:
                     e.activation(o, i, AF.Silu, scale=sc))
                k.se(lambda e, o=ust[:, :], i=k.pb(1, 512), sc=s1u[:, i:i + 1]:
                     e.activation(o, i, AF.Copy, scale=sc), inc=True)
                k.wait(k.v, k.eS)
                h_last_inc = k.ve(lambda e, o=h[:, i * 1024 + cc * 512:i * 1024 + (cc + 1) * 512],
                                  a=sgst[:, :], b=ust[:, :]: e.tensor_mul(o, a, b), inc=True)
        k.tag(f'l1ffn down H={H}')
        for o in range(16):
            sl = o % 2
            if dn_free[sl]:
                k.wait(k.g, k.mT, dn_free[sl])
            td = k.dma("g", dsl[sl][:, :], wd[o, :, :])
            k.wait(k.t, k.dG, td)
            k.wait(k.t, k.eV, h_last_inc)
            for cc in range(2):
                pbank = 2 + cc
                k.wait(k.t, k.eV)
                for i in range(44):
                    k.mm(k.pb(pbank, 512), dsl[sl][:, i * 128:(i + 1) * 128],
                         h[:, i * 1024 + cc * 512:i * 1024 + (cc + 1) * 512],
                         start=(i == 0), stop=(i == 43), inc=(i == 43))
                dn_free[sl] = k.mT.n
                k.wait(k.v, k.mT)
                k.ve(lambda e, o=k.xt(o, c0 + cc * 512, c0 + (cc + 1) * 512),
                     i0=k.pb(pbank, 512), sc=s1d[:, o:o + 1],
                     i1=k.xt(o, c0 + cc * 512, c0 + (cc + 1) * 512):
                     e.scalar_tensor_tensor(o, i0, sc, i1, op0=ALU.mult, op1=ALU.add),
                     inc=True)
        k.A.reset(m0)
        scr = ln_scratch(k)
        ph_ln(k, [c0, c0 + 512], scr)
    k.A.reset(m0)



def ph_ffn0(k):
    nc = k.nc
    wsrc = []
    for s in range(12):
        wsrc.append((k.din(f"w_fgu{s}", (11, 128, 2 * NT * 128), F32),
                     k.din(f"w_fd{s}", (16, 128, 11 * 128), F32)))
    m0 = k.A.mark()
    acc = k.A.take(F32, NT * 1024)
    rtb = k.A.take(BF16, 128)
    sgwb = k.A.take(BF16, 16)
    macc = k.A.mark()
    ident_f = k.consts["c_identf"]
    ones_row = k.consts["c_ones_row"]

    wrt = k.consts["w_rt"]
    wsg = k.consts["w_sg"]
    k.barrier()
    k.ve(lambda e, o=rtb[:, :], i=wrt[:, :]: e.tensor_copy(o, i))
    k.ve(lambda e, o=sgwb[:, :], i=wsg[:, :]: e.tensor_copy(o, i), inc=True)

    up_free = [0, 0]
    dn_free = [0, 0]
    for H in range(2):
        c0 = H * 1024
        k.A.reset(macc)
        k.wait(k.g, k.mT)
        k.wait(k.g, k.eV)
        k.wait(k.g, k.eS)
        h = k.A.take(BF16, 11 * 1024)
        gusl = [k.A.take(BF16, 2 * NT * 128) for _ in range(2)]
        dsl = [k.A.take(BF16, 11 * 128) for _ in range(2)]
        sgst = k.A.take(BF16, 512)
        ust = k.A.take(BF16, 512)
        lgsb = k.A.take(F32, 512)
        lgtm = k.A.take(F32, 4 * 8)
        prb = k.A.take(F32, 8)
        msk = k.A.take(F32, 8)
        pm2 = k.A.take(F32, 8)
        colv = k.A.take(F32, 4)
        cmbf = k.A.take(F32, 8)
        cmbb = k.A.take(BF16, 8)
        cmbT = k.A.take(BF16, 8 * 1024)
        sgrow = k.A.take(BF16, 1024)
        cbt = [k.A.take(BF16, 512) for _ in range(2)]
        rowD = k.A.take(F32, 512)
        k.tag(f'ffn0 router H={H}')
        for cc in range(2):
            cols = (c0 + cc * 512, c0 + (cc + 1) * 512)
            k.wait(k.t, k.eV)
            for t in range(NT):
                k.mm(k.pb(5, 512)[0:8, :], rtb[:, t * 8:(t + 1) * 8], k.xt(t, *cols),
                     start=(t == 0), stop=(t == NT - 1), inc=(t == NT - 1))
            k.wait(k.v, k.mT)
            k.ve(lambda e, o=lgsb[0:8, :], i=k.pb(5, 512)[0:8, :]: e.tensor_copy(o, i),
                 inc=True)
            if cc == 0:
                k.dump(f"lgsb_H{H}", lgsb[0:8, :], 512)
            for tt in range(4):
                k.wait(k.t, k.eV)
                k.tr(k.pb(6, 8), lgsb[0:8, tt * 128:(tt + 1) * 128], ident_f[0:8, 0:8], inc=True)
                k.wait(k.v, k.mT)
                k.ve(lambda e, o=lgtm[:, tt * 8:(tt + 1) * 8], i=k.pb(6, 8):
                     e.tensor_copy(o, i), inc=True)
            for tt in range(4):
                lg = lgtm[:, tt * 8:(tt + 1) * 8]
                k.ve(lambda e, o=colv[:, 0:1], i=lg: e.reduce_max(o, i, axis=AX.X), inc=True)
                k.wait(k.v, k.eV)
                k.ve(lambda e, o=colv[:, 0:1], i=colv[:, 0:1]: e.tensor_scalar_mul(o, i, -1.0),
                     inc=True)
                k.wait(k.s, k.eV)
                k.se(lambda e, o=prb[:, :], i=lg, b=colv[:, 0:1]:
                     e.activation(o, i, AF.Exp, bias=b), inc=True)
                k.wait(k.v, k.eS)
                k.ve(lambda e, o=colv[:, 1:2], i=prb[:, :]: e.reduce_max(o, i, axis=AX.X),
                     inc=True)
                k.wait(k.v, k.eV)
                def vs(fn):
                    k.ve(fn, inc=True)
                    k.wait(k.v, k.eV)
                vs(lambda e, o=msk[:, :], i=prb[:, :], s=colv[:, 1:2]:
                   e.tensor_single_scalar(o, i, s, op=ALU.is_ge))
                vs(lambda e, o=msk[:, :], i=msk[:, :]:
                   e.tensor_scalar(o, i, -1.0, 1.0, op0=ALU.mult, op1=ALU.add))
                vs(lambda e, o=pm2[:, :], a=prb[:, :], b=msk[:, :]: e.tensor_mul(o, a, b))
                vs(lambda e, o=colv[:, 2:3], i=pm2[:, :]: e.reduce_max(o, i, axis=AX.X))
                vs(lambda e, o=msk[:, :], i=prb[:, :], s=colv[:, 2:3]:
                   e.tensor_single_scalar(o, i, s, op=ALU.is_ge))
                vs(lambda e, o=colv[:, 3:4], a=colv[:, 1:2], b=colv[:, 2:3]:
                   e.tensor_add(o, a, b))
                vs(lambda e, o=colv[:, 3:4], i=colv[:, 3:4]: e.reciprocal(o, i))
                vs(lambda e, o=cmbf[:, :], a=prb[:, :], b=msk[:, :]: e.tensor_mul(o, a, b))
                vs(lambda e, o=cmbf[:, :], i=cmbf[:, :], s=colv[:, 3:4]:
                   e.tensor_scalar_mul(o, i, s))
                k.wait(k.v, k.mT)
                k.ve(lambda e, o=cmbb[:, :], i=cmbf[:, :]: e.tensor_copy(o, i), inc=True)
                if tt == 0 and cc == 0:
                    k.dump(f"colv4_H{H}", colv[:, 0:4], 4)
                if tt == 0 and cc == 0:
                    k.dump(f"cmbf_H{H}", cmbf[:, :], 8)
                    k.dump(f"prb_H{H}", prb[:, :], 8)
                    k.dump(f"lgtm_H{H}", lgtm[:, 0:8], 8)
                k.wait(k.t, k.eV)
                for ex in range(8):
                    pbank = 6 + (ex % 2)
                    k.wait(k.t, k.eS)
                    k.tr(k.pb(pbank, 128, BF16)[0:1, :], cmbb[:, ex:ex + 1],
                         k.consts["c_ident"][:], inc=True)
                    k.wait(k.s, k.mT)
                    k.se(lambda e,
                         o=cmbT[0:1, ex * 1024 + cc * 512 + tt * 128:
                                ex * 1024 + cc * 512 + (tt + 1) * 128],
                         i=k.pb(pbank, 128, BF16)[0:1, :]: e.copy(o, i), inc=True)
            k.wait(k.t, k.eV)
            for t in range(NT):
                k.mm(k.pb(5, 512)[0:1, :], sgwb[:, t:t + 1], k.xt(t, *cols),
                     start=(t == 0), stop=(t == NT - 1), inc=(t == NT - 1))
            k.wait(k.s, k.mT)
            k.wait(k.s, k.eV)
            k.se(lambda e, o=rowD[0:1, :], i=k.pb(5, 512)[0:1, :]:
                 e.activation(o, i, AF.Exp, scale=-1.0), inc=True)
            k.wait(k.v, k.eS)
            k.ve(lambda e, o=rowD[0:1, :], i=rowD[0:1, :]: e.tensor_scalar_add(o, i, 1.0))
            k.ve(lambda e, o=rowD[0:1, :], i=rowD[0:1, :]: e.reciprocal(o, i))
            k.ve(lambda e, o=sgrow[0:1, cc * 512:(cc + 1) * 512],
                 i=rowD[0:1, :]: e.tensor_copy(o, i), inc=True)

        k.tag(f'ffn0 acc init H={H}')
        for t in range(NT):
            k.ve(lambda e, o=acc[:, t * 1024:(t + 1) * 1024], i=k.xt(t, c0, c0 + 1024):
                 e.tensor_copy(o, i), inc=(t == NT - 1))

        for s in FFN0_SOURCES:
            k.tag(f'ffn0 src {s} H={H}')
            wgud, wdd = wsrc[s]
            for cc in range(2):
                k.wait(k.t, k.eV)
                k.wait(k.t, k.eS)
                if s < 4:
                    rhs = sgrow[0:1, cc * 512:(cc + 1) * 512]
                else:
                    ex = s - 4
                    rhs = cmbT[0:1, ex * 1024 + cc * 512:ex * 1024 + (cc + 1) * 512]
                k.mm(k.pb(4, 512), ones_row[:], rhs, start=True, stop=True, inc=True)
                k.wait(k.s, k.mT)
                k.se(lambda e, o=cbt[cc][:, :], i=k.pb(4, 512): e.copy(o, i), inc=True)
                if cc == 0:
                    k.dump(f"cbt{s}_H{H}", cbt[0][:, :], 512)
            h_last_inc = 0
            for i in range(11):
                sl = i % 2
                if up_free[sl]:
                    k.wait(k.g, k.mT, up_free[sl])
                tu = k.dma("g", gusl[sl][:, :], wgud[i, :, :])
                k.wait(k.t, k.dG, tu)
                goff = 0
                uoff = NT * 128
                for cc in range(2):
                    cols = (c0 + cc * 512, c0 + (cc + 1) * 512)
                    k.wait(k.t, k.eS)
                    for t in range(NT):
                        k.mm(k.pb(0, 512), gusl[sl][:, goff + t * 128:goff + (t + 1) * 128],
                             k.xt(t, *cols),
                             start=(t == 0), stop=(t == NT - 1), inc=(t == NT - 1))
                    for t in range(NT):
                        k.mm(k.pb(1, 512), gusl[sl][:, uoff + t * 128:uoff + (t + 1) * 128],
                             k.xt(t, *cols),
                             start=(t == 0), stop=(t == NT - 1), inc=(t == NT - 1))
                    up_free[sl] = k.mT.n
                    k.wait(k.s, k.mT)
                    k.wait(k.s, k.eV)
                    k.se(lambda e, o=sgst[:, :], i=k.pb(0, 512):
                         e.activation(o, i, AF.Silu))
                    k.se(lambda e, o=ust[:, :], i=k.pb(1, 512):
                         e.activation(o, i, AF.Copy), inc=True)
                    k.wait(k.v, k.eS)
                    if FFN0_SKIP_H:
                        h_last_inc = k.ve(
                            lambda e, o=h[:, i * 1024 + cc * 512:i * 1024 + (cc + 1) * 512],
                            i=sgst[:, :]: e.tensor_copy(o, i), inc=True)
                    else:
                        k.ve(lambda e, o=ust[:, :], a=sgst[:, :], b=ust[:, :]:
                             e.tensor_mul(o, a, b))
                        h_last_inc = k.ve(
                            lambda e, o=h[:, i * 1024 + cc * 512:i * 1024 + (cc + 1) * 512],
                            a=ust[:, :], b=cbt[cc][:, :]: e.tensor_mul(o, a, b), inc=True)
            k.dump(f"h{s}_H{H}", h[:, 0:512], 512)
            for o in ([] if FFN0_SKIP_DOWN else range(16)):
                sl = o % 2
                if dn_free[sl]:
                    k.wait(k.g, k.mT, dn_free[sl])
                td = k.dma("g", dsl[sl][:, :], wdd[o, :, :])
                k.wait(k.t, k.dG, td)
                k.wait(k.t, k.eV, h_last_inc)
                for cc in range(2):
                    pbank = 2 + cc
                    k.wait(k.t, k.eV)
                    for i in range(11):
                        k.mm(k.pb(pbank, 512), dsl[sl][:, i * 128:(i + 1) * 128],
                             h[:, i * 1024 + cc * 512:i * 1024 + (cc + 1) * 512],
                             start=(i == 0), stop=(i == 10), inc=(i == 10))
                    dn_free[sl] = k.mT.n
                    k.wait(k.v, k.mT)
                    k.ve(lambda e, o=acc[:, o * 1024 + cc * 512:o * 1024 + (cc + 1) * 512],
                         a=acc[:, o * 1024 + cc * 512:o * 1024 + (cc + 1) * 512],
                         b=k.pb(pbank, 512): e.tensor_tensor(o, a, b, op=ALU.add),
                         inc=True)
        k.dump(f"acc_H{H}", acc[:, 0:512], 512)
        k.dump(f"sg_H{H}", sgrow[0:1, 0:512], 512)
        for t in range(NT):
            k.ve(lambda e, o=k.xt(t, c0, c0 + 1024), i=acc[:, t * 1024:(t + 1) * 1024]:
                 e.tensor_copy(o, i), inc=(t == NT - 1))
        k.A.reset(macc)
        scr = ln_scratch(k)
        ph_ln(k, [c0, c0 + 512], scr)
    k.A.reset(m0)



def ph_l1attn(k):
    nc = k.nc
    w1q = k.din("w1_q", (4, 128, NT * 512), F32)
    w1k = k.din("w1_k", (8, 128, NT * 256), F32)
    w1v = k.din("w1_v", (8, 128, NT * 256), F32)
    w1p = k.din("w1_p", (4, 128, NT * 512), F32)
    b1p = k.consts["b1_p"]
    o_scr = nc.dram_tensor("l1_o_scr", [D, T], BF16, kind="Internal").ap()
    ident = k.consts["c_ident"]
    bo2 = k.consts["c_blockones2"]
    ones_col = k.consts["c_ones_col"]

    m0 = k.A.mark()
    qT = k.A.take(BF16, NT * T)
    m1 = k.A.mark()
    qsl = [k.A.take(BF16, NT * 512) for _ in range(2)]
    k.barrier()
    k.tag('l1a qproj')
    qfree = [0, 0]
    for c4 in range(4):
        sl = c4 % 2
        if qfree[sl]:
            k.wait(k.g, k.eS, qfree[sl])
        tq = k.dma("g", qsl[sl][:, :], w1q[c4, :, :])
        k.wait(k.t, k.dG, tq)
        for o in range(4):
            for cc in range(4):
                pbank = 2 + (cc % 2)
                k.wait(k.t, k.eS)
                for t in range(NT):
                    k.mm(k.pb(pbank, 512),
                         qsl[sl][:, t * 512 + o * 128:t * 512 + (o + 1) * 128],
                         k.xt(t, cc * 512, (cc + 1) * 512),
                         start=(t == 0), stop=(t == NT - 1), inc=(t == NT - 1))
                k.wait(k.s, k.mT)
                k.se(lambda e, o2=qT[:, (c4 * 4 + o) * T + cc * 512:(c4 * 4 + o) * T + (cc + 1) * 512],
                     i=k.pb(pbank, 512): e.activation(o2, i, AF.Relu), inc=True)
        qfree[sl] = k.eS.n
    k.A.reset(m1)
    k.wait(k.g, k.mT)
    k.wait(k.g, k.eV)
    k.wait(k.g, k.eS)
    ktm = k.A.take(BF16, NTT * 256)
    vtm = k.A.take(BF16, NTT * 256)
    ksl = [k.A.take(BF16, NT * 256) for _ in range(2)]
    vsl = [k.A.take(BF16, NT * 256) for _ in range(2)]
    bd = k.A.take(BF16, 80)
    histg = k.A.take(BF16, 34)
    lostg = k.A.take(F32, 34)
    qstg = k.A.take(BF16, T)
    drow = k.A.take(F32, 512)
    drecb = k.A.take(BF16, 512)
    bcb = k.A.take(BF16, 512)
    ostg = [k.A.take(BF16, 512) for _ in range(2)]
    kvfree = [0, 0]
    for c in range(8):
        k.tag(f'l1a kv c={c}')
        sl = c % 2
        if kvfree[sl]:
            k.wait(k.g, k.mT, kvfree[sl])
        tk = k.dma("g", ksl[sl][:, :], w1k[c, :, :])
        tv = k.dma("g", vsl[sl][:, :], w1v[c, :, :])
        k.wait(k.t, k.dG, tv)
        for tt in range(NTT):
            pbank = tt % 2
            k.wait(k.t, k.eS)
            k.wait(k.t, k.eV)
            for t in range(NT):
                k.mm(k.pb(pbank, 256), k.xt(t, tt * 128, (tt + 1) * 128),
                     ksl[sl][:, t * 256:(t + 1) * 256],
                     start=(t == 0), stop=(t == NT - 1), inc=(t == NT - 1))
            k.wait(k.s, k.mT)
            k.se(lambda e, o=ktm[:, tt * 256:(tt + 1) * 256], i=k.pb(pbank, 256):
                 e.activation(o, i, AF.Relu), inc=True)
            k.wait(k.t, k.eS)
            for t in range(NT):
                k.mm(k.pb(pbank, 256), k.xt(t, tt * 128, (tt + 1) * 128),
                     vsl[sl][:, t * 256:(t + 1) * 256],
                     start=(t == 0), stop=(t == NT - 1), inc=(t == NT - 1))
            k.wait(k.v, k.mT)
            k.ve(lambda e, o=vtm[:, tt * 256:(tt + 1) * 256], i=k.pb(pbank, 256):
                 e.tensor_copy(o, i), inc=True)
        kvfree[sl] = k.mT.n
        for p in range(4):
            gp = c * 4 + p
            for hh in range(2):
                pbank = 4 + hh
                hcol = (2 * p + hh) * 32
                k.wait(k.t, k.eV)
                k.wait(k.t, k.eS)
                for tt in range(NTT):
                    k.mm(k.pb(pbank, 34)[0:32, 0:32],
                         ktm[:, tt * 256 + hcol:tt * 256 + hcol + 32],
                         vtm[:, tt * 256 + hcol:tt * 256 + hcol + 32],
                         start=(tt == 0), stop=False)
                for tt in range(NTT):
                    k.mm(k.pb(pbank, 34)[0:32, 32:33],
                         ktm[:, tt * 256 + hcol:tt * 256 + hcol + 32],
                         ones_col[:],
                         start=False, stop=(tt == NTT - 1), inc=(tt == NTT - 1))
            k.wait(k.v, k.mT)
            k.ve(lambda e, o=bd[0:128, 0:66]: e.memset(o, 0.0))
            for hh in range(2):
                pbank = 4 + hh
                r0 = hh * 32
                k.ve(lambda e, o=histg[0:32, 0:33], i=k.pb(pbank, 34)[0:32, 0:33]:
                     e.tensor_copy(o, i), inc=True)
                k.wait(k.v, k.eV)
                k.ve(lambda e, o=lostg[0:32, 0:33], a=k.pb(pbank, 34)[0:32, 0:33],
                     b=histg[0:32, 0:33]: e.tensor_tensor(o, a, b, op=ALU.subtract),
                     inc=True)
                k.wait(k.v, k.eV)
                k.ve(lambda e, o=bd[r0:r0 + 32, r0:r0 + 32], i=histg[0:32, 0:32]:
                     e.tensor_copy(o, i))
                k.ve(lambda e, o=bd[r0:r0 + 32, 64 + hh:65 + hh], i=histg[0:32, 32:33]:
                     e.tensor_copy(o, i))
                k.ve(lambda e, o=bd[64 + r0:96 + r0, r0:r0 + 32], i=lostg[0:32, 0:32]:
                     e.tensor_copy(o, i))
                k.ve(lambda e, o=bd[64 + r0:96 + r0, 64 + hh:65 + hh], i=lostg[0:32, 32:33]:
                     e.tensor_copy(o, i))
            tile = (gp * 64) // 128
            off = (gp * 64) % 128
            k.ve(lambda e, o=qstg[0:64, :], i=qT[off:off + 64, tile * T:(tile + 1) * T]:
                 e.tensor_copy(o, i))
            k.ve(lambda e, o=qstg[64:128, :], i=qT[off:off + 64, tile * T:(tile + 1) * T]:
                 e.tensor_copy(o, i), inc=True)
            for cc in range(4):
                k.wait(k.t, k.eV)
                k.mm(k.pb(6, 512)[0:66, :], bd[0:128, 0:66],
                     qstg[0:128, cc * 512:(cc + 1) * 512],
                     start=True, stop=True, inc=True)
                k.wait(k.v, k.mT)
                k.ve(lambda e, o=drow[64:66, :], i=k.pb(6, 512)[64:66, :]:
                     e.tensor_scalar_add(o, i, 1e-15))
                k.ve(lambda e, o=drow[64:66, :], i=drow[64:66, :]: e.reciprocal(o, i))
                k.ve(lambda e, o=drecb[0:2, :], i=drow[64:66, :]: e.tensor_copy(o, i),
                     inc=True)
                k.wait(k.t, k.eV)
                k.mm(k.pb(7, 512)[0:64, :], bo2[:], drecb[0:2, :], start=True, stop=True,
                     inc=True)
                k.wait(k.s, k.mT)
                k.se(lambda e, o=bcb[0:64, :], i=k.pb(7, 512)[0:64, :]: e.copy(o, i),
                     inc=True)
                osl = cc % 2
                k.wait(k.v, k.eS)
                k.wait(k.v, k.dY)
                k.ve(lambda e, o=ostg[osl][0:64, :], a=k.pb(6, 512)[0:64, :], b=bcb[0:64, :]:
                     e.tensor_mul(o, a, b), inc=True)
                k.wait(k.y, k.eV)
                k.dma("y", o_scr[gp * 64:(gp + 1) * 64, cc * 512:(cc + 1) * 512],
                      ostg[osl][0:64, :])
    k.tag('l1a proj')
    k.A.reset(m1)
    osl_t = [k.A.take(BF16, NT * 512) for _ in range(2)]
    psl = [k.A.take(BF16, NT * 512) for _ in range(2)]
    k.barrier()
    for cc in range(4):
        so = cc % 2
        k.wait(k.y, k.mT)
        ot = []
        for t in range(NT):
            ot.append(k.dma("y", osl_t[so][:, t * 512:(t + 1) * 512],
                            o_scr[t * 128:(t + 1) * 128, cc * 512:(cc + 1) * 512]))
        for c4 in range(4):
            sp = c4 % 2
            k.wait(k.g, k.mT)
            tp = k.dma("g", psl[sp][:, :], w1p[c4, :, :])
            k.wait(k.t, k.dG, tp)
            k.wait(k.t, k.dY, ot[-1])
            for o in range(4):
                oo = c4 * 4 + o
                pbank = (oo % 2)
                k.wait(k.t, k.eV)
                for t in range(NT):
                    k.mm(k.pb(pbank, 512),
                         psl[sp][:, t * 512 + o * 128:t * 512 + (o + 1) * 128],
                         osl_t[so][:, t * 512:(t + 1) * 512],
                         start=(t == 0), stop=(t == NT - 1), inc=(t == NT - 1))
                k.wait(k.v, k.mT)
                k.ve(lambda e, o2=k.xt(oo, cc * 512, (cc + 1) * 512), i0=k.pb(pbank, 512),
                     sc=b1p[:, oo:oo + 1], i1=k.xt(oo, cc * 512, (cc + 1) * 512):
                     e.scalar_tensor_tensor(o2, i0, sc, i1, op0=ALU.add, op1=ALU.add),
                     inc=True)
    k.A.reset(m0)
    scr = ln_scratch(k)
    ph_ln(k, [0, 512, 1024, 1536], scr)
    k.A.reset(m0)



def ph_l0attn(k):
    nc = k.nc
    wq = k.din("w_q", (16, 128, NT * 128), BF16)
    wk = k.din("w_k", (4, 128, NT * 128), BF16)
    wv = k.din("w_v", (4, 128, NT * 128), BF16)
    wo = k.din("w_o", (16, 128, NT * 128), BF16)
    bq = k.consts["b_q"]
    bk = k.consts["b_k"]
    bv = k.consts["b_v"]
    cos_d = k.din("c_cos", (128, 2048), BF16)
    sin_d = k.din("c_sin", (128, 2048), BF16)
    ident = k.consts["c_ident"]
    mask = k.consts["c_mask"]
    ones_row = k.consts["c_ones_row"]
    o_scr = nc.dram_tensor("l0_o_scr", [D, T], BF16, kind="Internal").ap()

    m0 = k.A.mark()
    cosT = k.A.take(BF16, 2048)
    sinS = k.A.take(BF16, 2048)
    k.dma("y", cosT[:, :], cos_d)
    k.dma("y", sinS[:, :], sin_d)
    PT = k.A.take(BF16, NTT * T)
    Qb = [k.A.take(BF16, T) for _ in range(2)]
    Kb = k.A.take(BF16, T)
    Vb = k.A.take(BF16, NTT * 128)
    Pst = [k.A.take(BF16, T) for _ in range(2)]
    Srow = [k.A.take(BF16, T) for _ in range(2)]
    raw = k.A.take(BF16, 512)
    rot = k.A.take(BF16, 512)
    rp1 = k.A.take(BF16, 512)
    rp2 = k.A.take(BF16, 512)
    wsl = [k.A.take(BF16, NT * 128) for _ in range(2)]
    mx = k.A.take(F32, 8)
    Ost = [k.A.take(BF16, T) for _ in range(2)]
    k.barrier()

    def rope_evac(psum_ap, bias_col, out_ap, ccols):
        k.wait(k.v, k.mT)
        k.wait(k.v, k.eS)
        k.ve(lambda e, o=raw[:, :], i=psum_ap, b=bias_col:
             e.tensor_scalar_add(o, i, b), inc=True)
        k.wait(k.s, k.eV)
        k.se(lambda e, o=rot[0:64, :], i=raw[64:128, :]: e.copy(o, i))
        k.se(lambda e, o=rot[64:128, :], i=raw[0:64, :]: e.copy(o, i), inc=True)
        k.wait(k.v, k.eS)
        k.ve(lambda e, o=rp1[:, :], a=rot[:, :], b=sinS[:, ccols[0]:ccols[1]]:
             e.tensor_mul(o, a, b))
        k.ve(lambda e, o=rp2[:, :], a=raw[:, :], b=cosT[:, ccols[0]:ccols[1]]:
             e.tensor_mul(o, a, b))
        k.ve(lambda e, o=out_ap, a=rp2[:, :], b=rp1[:, :]: e.tensor_add(o, a, b), inc=True)

    def proj_head(slab, bias_col, outbuf):
        for cc in range(4):
            cols = (cc * 512, (cc + 1) * 512)
            k.wait(k.t, k.eS)
            k.wait(k.t, k.eV)
            for t in range(NT):
                k.mm(k.pb(4, 512), slab[:, t * 128:(t + 1) * 128], k.xt(t, *cols),
                     start=(t == 0), stop=(t == NT - 1), inc=(t == NT - 1))
            rope_evac(k.pb(4, 512), bias_col, outbuf[:, cols[0]:cols[1]], cols)

    for kv in range(4):
        k.tag(f'l0a kv={kv}')
        slk = wsl[0]
        k.wait(k.g, k.mT)
        tk = k.dma("g", slk[:, :], wk[kv, :, :])
        k.wait(k.t, k.dG, tk)
        proj_head(slk, bk[:, kv:kv + 1], Kb)
        slv = wsl[1]
        k.wait(k.g, k.mT)
        tv = k.dma("g", slv[:, :], wv[kv, :, :])
        k.wait(k.t, k.dG, tv)
        for tt in range(NTT):
            k.wait(k.t, k.eV)
            for t in range(NT):
                k.mm(k.pb(5, 128), k.xt(t, tt * 128, (tt + 1) * 128),
                     slv[:, t * 128:(t + 1) * 128],
                     start=(t == 0), stop=False)
            k.mm(k.pb(5, 128), ones_row[:], bv[0:1, kv * 128:(kv + 1) * 128],
                 start=False, stop=True, inc=True)
            k.wait(k.v, k.mT)
            k.ve(lambda e, o=Vb[:, tt * 128:(tt + 1) * 128], i=k.pb(5, 128):
                 e.tensor_copy(o, i), inc=True)
        for g in range(4):
            hidx = kv * 4 + g
            qb = Qb[g % 2]
            slq = wsl[(g % 2)]
            k.wait(k.g, k.mT)
            tq = k.dma("g", slq[:, :], wq[hidx, :, :])
            k.wait(k.t, k.dG, tq)
            proj_head(slq, bq[:, hidx:hidx + 1], qb)
            scopy_val = [0] * NTT
            pscale_val = [0] * NTT
            trevac_val = {6: 0, 7: 0}
            prev_tr = 0

            def emit_transposes(i):
                nonlocal prev_tr
                psl = Pst[i % 2]
                k.wait(k.t, k.eV, pscale_val[i])
                for jt in range(i + 1):
                    pbank = 6 + (jt % 2)
                    k.wait(k.t, k.eS, trevac_val[pbank])
                    k.tr(k.pb(pbank, 128, BF16), psl[:, jt * 128:(jt + 1) * 128],
                         ident[:], inc=True)
                    tr_inc = k.mT.n
                    k.wait(k.s, k.mT, tr_inc)
                    k.se(lambda e, o=PT[:, jt * T + i * 128:jt * T + (i + 1) * 128],
                         i=k.pb(pbank, 128, BF16): e.copy(o, i), inc=True)
                    trevac_val[pbank] = k.eS.n
                prev_tr = k.mT.n

            for i in range(NTT):
                L = (i + 1) * 128
                mxs = (i % 2) * 4
                srow = Srow[i % 2]
                if i >= 1:
                    k.wait(k.t, k.eV, scopy_val[i - 1])
                nsub = (L + 511) // 512
                for csub in range(nsub):
                    a = csub * 512
                    b = min(a + 512, L)
                    k.mm(k.pbs(0, 4)[:, a:b], qb[:, i * 128:(i + 1) * 128], Kb[:, a:b],
                         start=True, stop=True, inc=(csub == nsub - 1))
                sc_inc = k.mT.n
                if i >= 1:
                    emit_transposes(i - 1)
                k.wait(k.v, k.mT, sc_inc)
                k.ve(lambda e, o=srow[:, 0:L], i=k.pbs(0, 4)[:, 0:L]: e.tensor_copy(o, i),
                     inc=True)
                scopy_val[i] = k.eV.n
                k.ve(lambda e, o=srow[:, i * 128:L], a=srow[:, i * 128:L],
                     b=mask[:, :]: e.tensor_tensor(o, a, b, op=ALU.add))
                k.ve(lambda e, o=mx[:, mxs:mxs + 1], i=srow[:, 0:L]:
                     e.reduce_max(o, i, axis=AX.X), inc=True)
                k.wait(k.v, k.eV)
                k.ve(lambda e, o=mx[:, mxs + 1:mxs + 2], i=mx[:, mxs:mxs + 1]:
                     e.tensor_scalar_mul(o, i, -SCALE_L0), inc=True)
                neg_inc = k.eV.n
                k.wait(k.s, k.eV, neg_inc)
                psl = Pst[i % 2]
                k.se(lambda e, o=psl[:, 0:L], i=srow[:, 0:L], b=mx[:, mxs + 1:mxs + 2]:
                     e.activation(o, i, AF.Exp, bias=b, scale=SCALE_L0), inc=True)
                exp_inc = k.eS.n
                k.wait(k.v, k.eS, exp_inc)
                k.ve(lambda e, o=mx[:, mxs + 2:mxs + 3], i=psl[:, 0:L]:
                     e.reduce_sum(o, i, axis=AX.X), inc=True)
                k.wait(k.v, k.eV)
                k.ve(lambda e, o=mx[:, mxs + 3:mxs + 4], i=mx[:, mxs + 2:mxs + 3]:
                     e.reciprocal(o, i), inc=True)
                k.wait(k.v, k.eV)
                k.ve(lambda e, o=psl[:, 0:L], i=psl[:, 0:L], s=mx[:, mxs + 3:mxs + 4]:
                     e.tensor_single_scalar(o, i, s, op=ALU.mult), inc=True)
                pscale_val[i] = k.eV.n
            emit_transposes(NTT - 1)
            ost = Ost[g % 2]
            k.wait(k.t, k.eS)
            for ccb in range(4):
                last_jt = 4 * ccb + 3
                for jt in range(last_jt + 1):
                    off = max(0, (jt - 4 * ccb)) * 128
                    k.mm(k.pbs(0, 4)[:, ccb * 512 + off:(ccb + 1) * 512],
                         Vb[:, jt * 128:(jt + 1) * 128],
                         PT[:, jt * T + ccb * 512 + off:jt * T + (ccb + 1) * 512],
                         start=(jt == 0), stop=(jt == last_jt), inc=(jt == last_jt))
                k.wait(k.s, k.mT)
                if ccb == 0:
                    k.wait(k.s, k.dY)
                k.se(lambda e, o=ost[:, ccb * 512:(ccb + 1) * 512],
                     i=k.pbs(0, 4)[:, ccb * 512:(ccb + 1) * 512]: e.copy(o, i),
                     inc=True)
            k.wait(k.y, k.eS)
            k.dma("y", o_scr[hidx * 128:(hidx + 1) * 128, :], ost[:, :])
    k.tag('l0a wo')
    k.A.reset(m0)
    oslt = [k.A.take(BF16, NT * 512) for _ in range(2)]
    wosl = [k.A.take(BF16, NT * 128) for _ in range(2)]
    wfree = [0, 0]
    k.barrier()
    for cc in range(4):
        so = cc % 2
        k.wait(k.y, k.mT)
        ot = []
        for t in range(NT):
            ot.append(k.dma("y", oslt[so][:, t * 512:(t + 1) * 512],
                            o_scr[t * 128:(t + 1) * 128, cc * 512:(cc + 1) * 512]))
        for o in range(16):
            sw = o % 2
            if wfree[sw]:
                k.wait(k.g, k.mT, wfree[sw])
            tw = k.dma("g", wosl[sw][:, :], wo[o, :, :])
            k.wait(k.t, k.dG, tw)
            k.wait(k.t, k.dY, ot[-1])
            pbank = 4 + (o % 2)
            k.wait(k.t, k.eV)
            for t in range(NT):
                k.mm(k.pb(pbank, 512), wosl[sw][:, t * 128:(t + 1) * 128],
                     oslt[so][:, t * 512:(t + 1) * 512],
                     start=(t == 0), stop=(t == NT - 1), inc=(t == NT - 1))
            wfree[sw] = k.mT.n
            k.wait(k.v, k.mT)
            k.ve(lambda e, o2=k.xt(o, cc * 512, (cc + 1) * 512),
                 a=k.xt(o, cc * 512, (cc + 1) * 512), b=k.pb(pbank, 512):
                 e.tensor_tensor(o2, a, b, op=ALU.add), inc=True)
    k.A.reset(m0)
    scr = ln_scratch(k)
    ph_ln(k, [0, 512, 1024, 1536], scr)
    k.A.reset(m0)



BLOCK_CONSTS = {"l0attn": L0_CONSTS, "ffn0": FFN0_CONSTS, "l1attn": L1A_CONSTS,
                "l1ffn": L1F_CONSTS}
BLOCK_FN = {"l0attn": ph_l0attn, "ffn0": ph_ffn0, "l1attn": ph_l1attn, "l1ffn": ph_l1ffn}


def build_core_program(blocks, in_mode, out_mode):
    nc = bass.Bass("TRN2", target_bir_lowering=False, debug=False)
    ctx = ExitStack()
    k = K(nc, ctx)
    cl = list(CONSTS)
    for b in blocks:
        cl += BLOCK_CONSTS[b]
    load_consts(k, cl)
    if in_mode == "tm":
        ph_load_x_tm(k)
    else:
        ph_load_ff(k)
    k.barrier()
    for b in blocks:
        BLOCK_FN[b](k)
        k.barrier()
    if out_mode == "tm":
        ph_out_tm(k)
    else:
        ph_store_ff(k)
    emit_replay(nc, k)
    ctx.close()
    return nc, list(dict.fromkeys(k.inputs_used))


def emit_replay(nc, k):
    block = k.ctx.enter_context(nc.Block())
    block.tensor(lambda e: k.t.replay(e))
    block.vector(lambda e: k.v.replay(e))
    block.scalar(lambda e: k.s.replay(e))
    block.gpsimd(lambda e: k.g.replay(e))
    block.sync(lambda e: k.y.replay(e))


ALL_BLOCKS = ["l0attn", "ffn0", "l1attn", "l1ffn"]


def kernel(**inputs):
    pk = host_pack(inputs)
    x = np.asarray(inputs["L0_attn_x"])
    cur = [np.ascontiguousarray(x[c]).astype(BF, copy=False) for c in range(8)]
    nc, names = build_core_program(ALL_BLOCKS, "tm", "tm")
    in_maps = []
    for c in range(8):
        m = {}
        for n in names:
            m[n] = cur[c] if n == "xin" else pk[n]
        in_maps.append(m)
    res = run_bass_kernel_spmd(nc, in_maps, core_ids=list(range(8)))
    return np.stack([np.asarray(res.results[c]["xout"], dtype=np.float32) for c in range(8)])
